# version 21
# baseline (speedup 1.0000x reference)
import numpy as np
from contextlib import ExitStack

import concourse.bass as bass
import concourse.mybir as mybir
from concourse.bass_utils import run_bass_kernel_spmd

F32 = mybir.dt.float32
BF16 = mybir.dt.bfloat16
I32 = mybir.dt.int32
ALU = mybir.AluOpType
ACT = mybir.ActivationFunctionType
AX = mybir.AxisListType

P = 128
D = 1024
NCH = D // P
MAGIC = 12582912.0
EPS_LN1 = 1e-5
EPS_ATT = 1e-3
EPS_QK = 1e-6
EPS_F = 1e-5
QMAX = 127.0
THRESH = 0.7

ENGS = ("tensor", "vector", "scalar", "gpsimd", "sync")
_LAST_PROG = None


class Prog:

    def __init__(self):
        self.ops = {e: [] for e in ENGS}
        self.cnt = {}

    def emit(self, eng, fn, inc=None, inc_by=1):
        if inc is not None:
            self.cnt[inc] = self.cnt.get(inc, 0) + inc_by
            ev = (inc, self.cnt[inc])
        else:
            ev = None
        self.ops[eng].append(("OP", fn, inc, inc_by))
        return ev

    def wait(self, eng, ev):
        if ev is None:
            return
        sem, val = ev
        self.ops[eng].append(("WAIT", sem, val, None))

    def run(self, eng_name, eng, sems):
        for entry in self.ops[eng_name]:
            if entry[0] == "WAIT":
                eng.wait_ge(sems[entry[1]], entry[2])
            else:
                _, fn, inc, inc_by = entry
                inst = fn(eng)
                if inc is not None:
                    inst.then_inc(sems[inc], inc_by)


def build_nc(NT=32):
    L = NT * P
    nc = bass.Bass("TRN2", target_bir_lowering=False,
                   detect_race_conditions=False)

    x_d = nc.dram_tensor("x", [L, D], F32, kind="ExternalInput").ap()
    wdr = {}
    for nm in ("wq", "wk", "wv", "wq2", "wk2", "wv2", "wo", "fw"):
        wdr[nm] = nc.dram_tensor(nm, [D, D], F32, kind="ExternalInput").ap()
    vdr = {}
    for nm in ("bq", "bk", "bv", "bq2", "bk2", "bv2", "bo", "gam", "bet",
               "gamF", "betF"):
        vdr[nm] = nc.dram_tensor(nm, [D], F32, kind="ExternalInput").ap()
    out_d = nc.dram_tensor("out", [L, D], F32, kind="ExternalOutput").ap()
    sp_xn = nc.dram_tensor("sp_xn", [L, D], F32).ap()
    sp_xm = nc.dram_tensor("sp_xm", [L, D], F32).ap()
    cc_in = nc.dram_tensor("cc_in", [1, 8], F32).ap()
    cc_out = nc.dram_tensor("cc_out", [1, 8], F32).ap()

    x_t = x_d.rearrange("(n p) d -> n p d", p=P)
    out_t = out_d.rearrange("(n p) d -> n p d", p=P)
    xn_t = sp_xn.rearrange("(n p) d -> n p d", p=P)
    xm_t = sp_xm.rearrange("(n p) d -> n p d", p=P)

    pr = Prog()
    global _LAST_PROG
    _LAST_PROG = pr

    with ExitStack() as ctx:
        E = ctx.enter_context

        wqq = E(nc.sbuf_tensor("wqq", [P, NCH * D], BF16))
        wkk = E(nc.sbuf_tensor("wkk", [P, NCH * D], BF16))
        wvv = E(nc.sbuf_tensor("wvv", [P, NCH * D], BF16))
        wo_s = E(nc.sbuf_tensor("wo_s", [P, NCH * D], BF16))
        sT = E(nc.sbuf_tensor("sT", [P, NCH * D], BF16))
        vT = E(nc.sbuf_tensor("vT", [P, max(NT - 1, 18) * D], BF16))
        gam_b = E(nc.sbuf_tensor("gam_b", [P, D], BF16))
        bet_b = E(nc.sbuf_tensor("bet_b", [P, D], BF16))
        gamF_b = E(nc.sbuf_tensor("gamF_b", [P, D], BF16))
        betF_b = E(nc.sbuf_tensor("betF_b", [P, D], BF16))
        pf = [E(nc.sbuf_tensor(f"pf{j}", [P, D], F32)) for j in range(8)]
        pb = [E(nc.sbuf_tensor(f"pb{j}", [P, D], BF16)) for j in range(5)]
        i32h = E(nc.sbuf_tensor("i32h", [P, 512], I32))
        S_sb = E(nc.sbuf_tensor("S_sb", [P, D], BF16))
        brep = E(nc.sbuf_tensor("brep", [P, 4 * D], BF16))
        ones128 = E(nc.sbuf_tensor("ones128", [P, P], BF16))
        ones_f = E(nc.sbuf_tensor("ones_f", [1, P], F32))
        idn_bf = E(nc.sbuf_tensor("idn_bf", [P, P], BF16))
        idn_f = E(nc.sbuf_tensor("idn_f", [P, P], F32))
        bq_bf = E(nc.sbuf_tensor("bq_bf", [P, 3 * NCH], BF16))
        bqb = pb[3]
        vflush = E(nc.sbuf_tensor("vflush", [P, P], F32))
        svYt = E(nc.sbuf_tensor("svYt", [P, 16], F32))
        svMt = E(nc.sbuf_tensor("svMt", [P, 16], F32))
        svY = [{}, {}]
        svM = [{}, {}]
        _svnames = ("xsum", "sumsq", "m", "m2", "var0", "var", "lnv", "negmr")
        for par in range(2):
            for ix, nm_ in enumerate(_svnames):
                svY[par][nm_] = svYt[:, par * 8 + ix: par * 8 + ix + 1]
                svM[par][nm_] = svMt[:, par * 8 + ix: par * 8 + ix + 1]
        rstd_all = E(nc.sbuf_tensor("rstd_all", [P, NT], F32))
        nm_all = E(nc.sbuf_tensor("nm_all", [P, NT], F32))
        sv = {}
        for nm_ in ("xsum", "m", "m2", "var", "var0", "dmy", "lnv", "negmr", "sumsq",
                    "wabs", "asum", "cnt", "delta_v", "ndelta_v", "tpart",
                    "alpha1", "gmax1", "gmaxc", "invs1", "as1",
                    "runmax", "amax_t", "invs_v", "as_v", "e5", "e3", "e10",
                    "s1p", "nh", "n1", "sjk", "eqk", "mgk"):
            sv[nm_] = E(nc.sbuf_tensor("sv_" + nm_, [P, 1], F32))
        red8 = E(nc.sbuf_tensor("sv_red8", [1, 8], F32))
        pack2 = E(nc.sbuf_tensor("sv_pack2", [1, 2], F32))

        ps_s = E(nc.psum_tensor("ps_s", [P, D], F32))
        ps_a = E(nc.psum_tensor("ps_a", [P, D], F32))
        ps_b = E(nc.psum_tensor("ps_b", [P, D], F32))
        ps_c = E(nc.psum_tensor("ps_c", [P, D], F32))
        psums = [ps_s, ps_a, ps_b, ps_c]

        sem_names = ["dx", "ds", "dg", "pe", "v", "sc", "g", "cc"]
        sems = {n: E(nc.semaphore("sem_" + n)) for n in sem_names}
        block = E(nc.Block())

        def vTs(j):
            return pb[3][:, :] if j == NT - 1 else vT[:, j * D:(j + 1) * D]

        WT_scr = vT[:, 0:8192]
        W2_scr = vT[:, 8192:16384]
        wcbuf = vT[:, 16384:17408]

        def bfv(ps, bank):
            return ps[:, bank * 512:(bank + 1) * 512].bitcast(BF16)

        def vemit(fn, inc="v"):
            pr.emit("vector", fn)
            return pr.emit("vector", lambda v: v.tensor_copy(
                out=vflush[:, :], in_=ones128[:, :]), inc=inc)

        ev_iota = pr.emit("gpsimd", lambda g: g.iota(
            vflush[:, :], pattern=[[-1, P]], base=0, channel_multiplier=1,
            allow_small_or_imprecise_dtypes=True), inc="g")


        pr.wait("vector", ev_iota)

        def _consts(v):
            v.memset(ones128[:, :], 1.0)
            v.memset(ones_f[0:1, :], 1.0)
            v.memset(sv["e5"][:, :], EPS_LN1)
            v.memset(sv["e3"][:, :], EPS_ATT)
            v.memset(sv["e10"][:, :], EPS_LN1 * EPS_F)
            v.memset(sv["s1p"][:, :], 1.0 + EPS_F)
            v.memset(sv["nh"][:, :], -0.5)
            v.memset(sv["n1"][:, :], -1.0)
            v.memset(sv["eqk"][:, :], EPS_QK)
            v.memset(sv["mgk"][:, :], 256.0)
            v.memset(sv["wabs"][:, :], 0.0)
            v.memset(sv["asum"][:, :], 0.0)
            v.memset(sv["cnt"][:, :], 0.0)
            v.memset(S_sb[:, :], 0.0)
            v.memset(sv["runmax"][:, :], 0.0)
            v.memset(red8[0:1, :], 0.0)
            v.tensor_scalar(out=idn_f[:, :], in0=vflush[:, :], scalar1=0.0,
                            scalar2=None, op0=ALU.is_equal)
            v.tensor_scalar(out=idn_bf[:, :], in0=vflush[:, :], scalar1=0.0,
                            scalar2=None, op0=ALU.is_equal)
            return v.tensor_copy(out=sv["dmy"][:, :], in_=sv["e5"][:, :])
        ev_consts = vemit(_consts)
        ev_idn = ev_consts

        pr.wait("gpsimd", ev_consts)
        ev_ccz = pr.emit("gpsimd", lambda g: g.dma_start(
            out=cc_in[0:1, :], in_=red8[0:1, :]), inc="dg", inc_by=16)
        pr.wait("gpsimd", ev_ccz)
        ev_ccw = pr.emit("gpsimd", lambda g: g.collective_compute(
            "AllReduce", ALU.max, replica_groups=[list(range(8))],
            ins=[cc_in.opt()], outs=[cc_out.opt()]), inc="cc")

        ev_rows = None
        for j, nm in enumerate(("gam", "bet", "gamF", "betF")):
            ev_rows = pr.emit("sync", lambda s, nm=nm, j=j: s.dma_start(
                out=pf[j][0:1, :], in_=vdr[nm][None, :]), inc="dx", inc_by=16)

        ev_bo_row = pr.emit("sync", lambda s: s.dma_start(
            out=pf[6][0:1, :], in_=vdr["bo"][None, :]), inc="dx", inc_by=16)
        ev_bsm = None
        for j, nm in enumerate(("bq", "bk", "bv")):
            ev_bsm = pr.emit("gpsimd", lambda g, nm=nm, j=j: g.dma_start(
                out=bq_bf[:, j * NCH:(j + 1) * NCH],
                in_=vdr[nm].rearrange("(c p) -> p c", p=P),
                allow_slow_non_contiguous=True), inc="dg", inc_by=16)
        ev_wo = pr.emit("gpsimd", lambda g: g.dma_start(
            out=wo_s[:, :].rearrange("p (c j) -> p c j", c=NCH),
            in_=wdr["wo"].rearrange("(c p) j -> p c j", p=P)), inc="dg", inc_by=16)

        pr.wait("tensor", ev_consts)
        pr.wait("tensor", ev_idn)
        pr.wait("tensor", ev_rows)
        bdsts = [gam_b, bet_b, gamF_b, betF_b]
        ev_bcast_mm = {}
        ev_bc = None
        for j in range(4):
            pr.wait("tensor", ev_bc)
            pr.emit("tensor", lambda t: t.drain())
            ev_mm = None
            for half in range(2):
                ev_mm = pr.emit("tensor", lambda t, j=j, half=half: t.matmul(
                    ps_s[:, half * 512:(half + 1) * 512], ones_f[0:1, :],
                    pf[j][0:1, half * 512:(half + 1) * 512],
                    start=True, stop=True, skip_group_check=True), inc="pe")
            pr.emit("tensor", lambda t: t.drain())
            ev_bcast_mm[j] = ev_mm
            pr.wait("vector", ev_mm)
            ev_bc = pr.emit("vector", lambda v, j=j: v.tensor_copy(
                out=bdsts[j][:, :], in_=ps_s[:, :]), inc="v")
        ev_bcast_evac = ev_bc

        pr.wait("tensor", ev_bo_row)
        pr.emit("tensor", lambda t: t.drain())
        ev_bom = None
        for half in range(2):
            ev_bom = pr.emit("tensor", lambda t, half=half: t.matmul(
                ps_a[:, half * 512:(half + 1) * 512], ones_f[0:1, :],
                pf[6][0:1, half * 512:(half + 1) * 512],
                start=True, stop=True, skip_group_check=True), inc="pe")
        pr.emit("tensor", lambda t: t.drain())
        pr.wait("vector", ev_bom)
        ev_borep = vemit(lambda v: v.tensor_scalar(
            out=brep[:, 3 * D:4 * D], in0=ps_a[:, :], scalar1=1.0 / P,
            scalar2=None, op0=ALU.mult))

        ev_combine_done = []
        ev_prev_w2_reader = None
        ev_prev_b2m = None
        ev_brow = None
        for widx, (w1n, w2n, wdst) in enumerate(
                (("wq", "wq2", wqq), ("wk", "wk2", wkk), ("wv", "wv2", wvv))):
            pr.wait("gpsimd", ev_prev_w2_reader)
            ev_w2 = pr.emit("gpsimd", lambda g, w2n=w2n: g.dma_start(
                out=W2_scr.rearrange("p (c j) -> p c j", c=NCH),
                in_=wdr[w2n].rearrange("(c p) j -> p c j", p=P)),
                inc="dg", inc_by=16)
            prev_evac = [ev_prev_w2_reader, ev_prev_w2_reader]
            prev_tp_last = None
            for ic in range(NCH):
                pr.wait("gpsimd", prev_tp_last)
                ev_chunk = pr.emit("gpsimd", lambda g, w1n=w1n, ic=ic: g.dma_start(
                    out=wcbuf.rearrange("p (o j) -> p o j", o=1),
                    in_=wdr[w1n][ic * P:(ic + 1) * P, :].rearrange(
                        "p (o j) -> p o j", o=1)), inc="dg", inc_by=16)
                pr.wait("tensor", ev_chunk)
                for batch in range(2):
                    pr.wait("tensor", prev_evac[batch])
                    ev_tp = None
                    for jj in range(4):
                        jc = batch * 4 + jj
                        ev_tp = pr.emit("tensor", lambda t, jc=jc, jj=jj, batch=batch: t.matmul(
                            ps_c[:, batch * 512 + jj * P:batch * 512 + (jj + 1) * P],
                            wcbuf[:, jc * P:(jc + 1) * P], idn_bf[:, :],
                            start=True, stop=True, skip_group_check=True), inc="pe")
                    prev_tp_last = ev_tp
                    pr.wait("scalar", ev_tp)
                    ev_ev = pr.emit("scalar", lambda s, ic=ic, batch=batch: s.copy(
                        out=WT_scr.rearrange("p (j i) -> p j i", j=NCH)[
                            :, batch * 4:(batch + 1) * 4, ic * P:(ic + 1) * P],
                        in_=ps_c[:, batch * 512:batch * 512 + 512].rearrange(
                            "p (j i) -> p j i", j=4)), inc="sc")
                    prev_evac[batch] = ev_ev
            pr.wait("tensor", ev_w2)
            pr.wait("tensor", prev_evac[0])
            pr.wait("tensor", prev_evac[1])
            ev_evac_prev = {}
            ev_last_combine_mm = None
            for rnd in range(2):
                evs_last = {}
                for jc in range(NCH):
                    for ibl in range(4):
                        ib = rnd * 4 + ibl
                        for half in range(2):
                            slot = ibl * 2 + half
                            psl = psums[slot // 2][
                                :, (slot % 2) * 512:(slot % 2) * 512 + 512]
                            if jc == 0:
                                pr.wait("tensor", ev_evac_prev.get(slot))
                            ev_mm = pr.emit("tensor", lambda t, psl=psl, jc=jc, ib=ib, half=half, wdst=wdst: t.matmul(
                                psl,
                                WT_scr[:, jc * D + ib * P: jc * D + (ib + 1) * P],
                                W2_scr[:, jc * D + half * 512: jc * D + half * 512 + 512],
                                start=(jc == 0), stop=(jc == NCH - 1),
                                skip_group_check=True), inc="pe")
                            evs_last[slot] = ev_mm
                ev_last_combine_mm = ev_mm
                for ibl in range(4):
                    ib = rnd * 4 + ibl
                    for half in range(2):
                        slot = ibl * 2 + half
                        psl = psums[slot // 2][
                            :, (slot % 2) * 512:(slot % 2) * 512 + 512]
                        pr.wait("vector", evs_last[slot])
                        ev_e = pr.emit("vector", lambda v, psl=psl, wdst=wdst, ib=ib, half=half: v.tensor_copy(
                            out=wdst[:, ib * D + half * 512: ib * D + half * 512 + 512],
                            in_=psl), inc="v")
                        ev_evac_prev[slot] = ev_e
                        ev_combine_done.append(ev_e)
            pr.wait("vector", ev_prev_w2_reader)
            pr.wait("vector", ev_bsm)
            ev_bqb = vemit(lambda v, widx=widx: v.tensor_copy(
                out=bqb[:, :].rearrange("p (c m) -> p m c", c=NCH),
                in_=bq_bf[:, widx * NCH:(widx + 1) * NCH].unsqueeze(1).broadcast_to(
                    [P, P, NCH])))
            b2n = ("bq2", "bk2", "bv2")[widx]
            pr.wait("sync", ev_prev_b2m)
            ev_b2 = pr.emit("sync", lambda s, b2n=b2n: s.dma_start(
                out=pf[5][0:1, :], in_=vdr[b2n][None, :]), inc="dx", inc_by=16)
            pr.wait("tensor", ev_b2)
            pr.wait("tensor", ev_evac_prev.get(2))
            pr.wait("tensor", ev_evac_prev.get(3))
            if widx == 0:
                pr.wait("tensor", ev_borep)
            pr.emit("tensor", lambda t: t.drain())
            ev_b2m = None
            for half in range(2):
                ev_b2m = pr.emit("tensor", lambda t, half=half: t.matmul(
                    ps_a[:, half * 512:(half + 1) * 512], ones_f[0:1, :],
                    pf[5][0:1, half * 512:(half + 1) * 512],
                    start=True, stop=True, skip_group_check=True), inc="pe")
            pr.emit("tensor", lambda t: t.drain())
            ev_prev_b2m = ev_b2m
            pr.wait("vector", ev_b2m)
            ev_b2r = vemit(lambda v: v.tensor_scalar(
                out=pf[6][:, :], in0=ps_a[:, :], scalar1=1.0 / P,
                scalar2=None, op0=ALU.mult))
            pr.wait("tensor", ev_bqb)
            pr.wait("tensor", ev_evac_prev[0])
            pr.wait("tensor", ev_evac_prev[1])
            evb = None
            for jc in range(NCH):
                for half in range(2):
                    psl = ps_s[:, half * 512:half * 512 + 512]
                    evb = pr.emit("tensor", lambda t, psl=psl, jc=jc, half=half: t.matmul(
                        psl, bqb[:, jc * P:(jc + 1) * P],
                        W2_scr[:, jc * D + half * 512: jc * D + half * 512 + 512],
                        start=(jc == 0), stop=(jc == NCH - 1),
                        skip_group_check=True), inc="pe" if jc == NCH - 1 else None)
            ev_prev_w2_reader = evb
            pr.wait("vector", evb)
            pr.wait("vector", ev_b2r)

            def _bias_fin(v, widx=widx):
                return v.scalar_tensor_tensor(
                    out=brep[:, widx * D:(widx + 1) * D], in0=ps_s[:, :],
                    scalar=1.0 / P, in1=pf[6][:, :], op0=ALU.mult, op1=ALU.add)
            ev_brow = vemit(_bias_fin)
            ev_combine_done.append(ev_brow)
            pr.wait("tensor", ev_brow)

        ev_p1 = {}
        ev_red = None
        for och in range(NCH):
            cc = och % 2
            buf = pf[6 + cc]
            pr.wait("sync", ev_p1.get(cc))
            if och < 2:
                pr.wait("sync", ev_brow)
            ev_l = pr.emit("sync", lambda s, och=och, buf=buf: s.dma_start(
                out=buf[:, :], in_=wdr["fw"][och * P:(och + 1) * P, :]),
                inc="dx", inc_by=16)
            pr.wait("vector", ev_l)

            def _red(v, buf=buf):
                v.reduce_sum(sv["m2"][:, :], buf[:, :], axis=AX.X,
                             apply_absolute_value=True)
                v.tensor_copy(out=sv["dmy"][:, :], in_=sv["e5"][:, :])
                return v.tensor_tensor(out=sv["wabs"][:, :], in0=sv["wabs"][:, :],
                                       in1=sv["m2"][:, :], op=ALU.add)
            ev_red = pr.emit("vector", _red, inc="v")
            ev_p1[cc] = ev_red
        pr.wait("tensor", ev_red)
        pr.wait("tensor", ev_brow)
        ev_tp = pr.emit("tensor", lambda t: t.transpose(
            ps_b[0:1, 0:P], sv["wabs"][:, 0:1], idn_f[:, :]), inc="pe")
        pr.wait("vector", ev_tp)

        def _delta(v):
            v.reduce_sum(sv["tpart"][0:1, 0:1], ps_b[0:1, 0:P], axis=AX.X)
            v.tensor_copy(out=sv["dmy"][:, :], in_=sv["e5"][:, :])
            return v.tensor_scalar(out=sv["tpart"][0:1, 0:1],
                                   in0=sv["tpart"][0:1, 0:1],
                                   scalar1=THRESH / (D * D), scalar2=None,
                                   op0=ALU.mult)
        ev_dl = vemit(_delta)
        pr.wait("tensor", ev_dl)
        pr.emit("tensor", lambda t: t.drain())
        ev_dbc = pr.emit("tensor", lambda t: t.matmul(
            ps_b[:, 0:1], ones_f[0:1, :], sv["tpart"][0:1, 0:1],
            start=True, stop=True, skip_group_check=True), inc="pe")
        pr.emit("tensor", lambda t: t.drain())
        pr.wait("vector", ev_dbc)

        def _dvec(v):
            v.tensor_copy(out=sv["delta_v"][:, :], in_=ps_b[:, 0:1])
            v.tensor_copy(out=sv["dmy"][:, :], in_=sv["e5"][:, :])
            return v.tensor_scalar(out=sv["ndelta_v"][:, :], in0=sv["delta_v"][:, :],
                                   scalar1=-1.0, scalar2=None, op0=ALU.mult)
        ev_dv = pr.emit("vector", _dvec, inc="v")

        prev_sT_evac = [None, None]
        prev_s_tp = None
        ev_stat = None
        ev_sgn = None
        for och in range(NCH):
            cc = och % 2
            buf = pf[6 + cc]
            pr.wait("sync", ev_stat if och >= 2 else ev_red)
            ev_l = pr.emit("sync", lambda s, och=och, buf=buf: s.dma_start(
                out=buf[:, :], in_=wdr["fw"][och * P:(och + 1) * P, :]),
                inc="dx", inc_by=16)
            pr.wait("vector", ev_l)
            pr.wait("vector", ev_dv)
            pr.wait("vector", prev_s_tp)

            def _sgn(v, buf=buf):
                v.tensor_scalar(out=pf[5][:, :], in0=buf[:, :],
                                scalar1=sv["delta_v"][:, 0:1], scalar2=None,
                                op0=ALU.is_gt)
                v.tensor_scalar(out=pf[0][:, :], in0=buf[:, :],
                                scalar1=sv["ndelta_v"][:, 0:1], scalar2=None,
                                op0=ALU.is_lt)
                return v.tensor_tensor(out=pb[0][:, :], in0=pf[5][:, :],
                                       in1=pf[0][:, :], op=ALU.subtract)
            ev_sgn = vemit(_sgn)

            def _stats(v, buf=buf):
                v.tensor_tensor(out=pf[5][:, :], in0=buf[:, :], in1=pb[0][:, :],
                                op=ALU.mult)
                v.reduce_sum(sv["m2"][:, :], pf[5][:, :], axis=AX.X)
                v.reduce_sum(sv["var0"][:, :], pb[0][:, :], axis=AX.X,
                             apply_absolute_value=True)
                v.tensor_tensor(out=sv["asum"][:, :], in0=sv["asum"][:, :],
                                in1=sv["m2"][:, :], op=ALU.add)
                return v.tensor_tensor(out=sv["cnt"][:, :], in0=sv["cnt"][:, :],
                                       in1=sv["var0"][:, :], op=ALU.add)
            ev_stat = pr.emit("vector", _stats, inc="v")
            pr.wait("tensor", ev_sgn)
            for batch in range(2):
                pr.wait("tensor", prev_sT_evac[batch])
                ev_tp = None
                for jj in range(4):
                    icb = batch * 4 + jj
                    ev_tp = pr.emit("tensor", lambda t, icb=icb, jj=jj, batch=batch: t.matmul(
                        ps_c[:, batch * 512 + jj * P:batch * 512 + (jj + 1) * P],
                        pb[0][:, icb * P:(icb + 1) * P], idn_bf[:, :],
                        start=True, stop=True, skip_group_check=True), inc="pe")
                prev_s_tp = ev_tp
                pr.wait("scalar", ev_tp)
                ev_ev = pr.emit("scalar", lambda s, och=och, batch=batch: s.copy(
                    out=sT.rearrange("p (i o) -> p i o", i=NCH)[
                        :, batch * 4:(batch + 1) * 4, och * P:(och + 1) * P],
                    in_=ps_c[:, batch * 512:batch * 512 + 512].rearrange(
                        "p (i o) -> p i o", i=4)), inc="sc")
                prev_sT_evac[batch] = ev_ev
        ev_sT_done = list(prev_sT_evac)
        ev_sT_tp_last = prev_s_tp

        pr.wait("tensor", ev_stat)
        pr.emit("tensor", lambda t: t.transpose(
            ps_b[0:1, 0:P], sv["asum"][:, 0:1], idn_f[:, :]), inc="pe")
        ev_tp3 = pr.emit("tensor", lambda t: t.transpose(
            ps_b[0:1, 128:256], sv["cnt"][:, 0:1], idn_f[:, :]), inc="pe")
        pr.wait("vector", ev_tp3)

        def _alpha_v(v):
            v.reduce_sum(sv["m2"][0:1, 0:1], ps_b[0:1, 128:256], axis=AX.X)
            v.reduce_sum(sv["alpha1"][0:1, 0:1], ps_b[0:1, 0:P], axis=AX.X)
            v.tensor_scalar(out=sv["m2"][0:1, 0:1], in0=sv["m2"][0:1, 0:1],
                            scalar1=1.0, scalar2=None, op0=ALU.max)
            return v.tensor_copy(out=sv["dmy"][:, :], in_=sv["e5"][:, :])
        ev_alphav = pr.emit("vector", _alpha_v, inc="v")
        pr.wait("scalar", ev_alphav)

        def _alpha_s(s):
            s.activation(out=sv["sjk"][0:1, 0:1], in_=sv["m2"][0:1, 0:1],
                         func=ACT.Ln)
            return s.activation(out=sv["m2"][0:1, 0:1], in_=sv["sjk"][0:1, 0:1],
                                func=ACT.Exp, scale=sv["n1"][0:1, 0:1])
        ev_alphas = pr.emit("scalar", _alpha_s, inc="sc")
        pr.wait("vector", ev_alphas)
        ev_alpha = vemit(lambda v: v.tensor_scalar(
            out=sv["alpha1"][0:1, 0:1], in0=sv["alpha1"][0:1, 0:1],
            scalar1=sv["m2"][0:1, 0:1], scalar2=None, op0=ALU.mult))

        RB = pf[3]
        evA = {}

        def A(i, key):
            return evA.get((i, key))

        def xTbuf(j):
            return pb[1] if j % 2 == 0 else pb[4]

        def emit_stats(j, dest_ev_prefix="A"):
            xb = pf[j % 2]
            pr.wait("scalar", A(j, "dx"))
            pr.wait("scalar", A(j - 1, "v_var"))
            pr.emit("scalar", lambda s, xb=xb, j=j: s.activation(
                out=vT[:, min(j, NT - 2) * D:min(j, NT - 2) * D + D],
                in_=xb[:, :], func=ACT.Square,
                accum_out=sv["sumsq"][:, 0:1]))
            evA[(j, "s_sq")] = pr.emit("scalar", lambda s, xb=xb, j=j: s.activation(
                out=vT[:, min(j, NT - 2) * D:min(j, NT - 2) * D + D],
                in_=xb[:, :], func=ACT.Identity,
                accum_out=sv["xsum"][:, 0:1]), inc="sc")
            pr.wait("vector", A(j, "s_sq"))

            def _smalls(v):
                v.tensor_scalar(out=sv["m"][:, :], in0=sv["xsum"][:, :],
                                scalar1=1.0 / D, scalar2=None, op0=ALU.mult)
                v.tensor_scalar(out=sv["var0"][:, :], in0=sv["sumsq"][:, :],
                                scalar1=1.0 / D, scalar2=None, op0=ALU.mult)
                v.tensor_scalar(out=sv["m2"][:, :], in0=sv["m"][:, :],
                                scalar1=sv["m"][:, 0:1], scalar2=None, op0=ALU.mult)
                v.tensor_copy(out=sv["dmy"][:, :], in_=sv["e5"][:, :])
                return v.tensor_tensor(out=sv["var"][:, :], in0=sv["var0"][:, :],
                                       in1=sv["m2"][:, :], op=ALU.subtract)
            evA[(j, "v_var")] = pr.emit("vector", _smalls, inc="v")
            pr.wait("scalar", A(j, "v_var"))

            def _rstd(s, j=j):
                s.activation(out=sv["lnv"][:, :], in_=sv["var"][:, :], func=ACT.Ln,
                             bias=sv["e5"][:, 0:1])
                return s.activation(out=rstd_all[:, j:j + 1], in_=sv["lnv"][:, :],
                                    func=ACT.Exp, scale=sv["nh"][:, 0:1])
            evA[(j, "s_rstd")] = pr.emit("scalar", _rstd, inc="sc")
            pr.wait("vector", A(j, "s_rstd"))
            pr.wait("vector", A(j - 1, "pe_x1t"))
            if j == 0:
                pr.wait("vector", ev_sT_tp_last)

            def _x1(v, j=j, xb=xb):
                v.tensor_scalar(out=nm_all[:, j:j + 1], in0=sv["m"][:, :],
                                scalar1=rstd_all[:, j:j + 1], scalar2=-1.0,
                                op0=ALU.mult, op1=ALU.mult)
                v.tensor_copy(out=sv["dmy"][:, :], in_=sv["e5"][:, :])
                return v.tensor_scalar(out=pb[0][:, :], in0=xb[:, :],
                                       scalar1=rstd_all[:, j:j + 1],
                                       scalar2=nm_all[:, j:j + 1],
                                       op0=ALU.mult, op1=ALU.add)
            evA[(j, "v_x1")] = vemit(_x1)

        def emit_x1t(j, bank_ps):
            ev_tp = None
            for icb in range(NCH):
                ev_tp = pr.emit("tensor", lambda t, icb=icb, bank_ps=bank_ps: t.matmul(
                    bank_ps[:, icb * P:(icb + 1) * P],
                    pb[0][:, icb * P:(icb + 1) * P], idn_bf[:, :],
                    start=True, stop=True, skip_group_check=True), inc="pe")
            evA[(j, "pe_x1t")] = ev_tp
            pr.wait("vector", ev_tp)
            evA[(j, "s_x1t")] = vemit(lambda v, j=j, bank_ps=bank_ps: v.tensor_copy(
                out=xTbuf(j)[:, :], in_=bank_ps[:, :]))

        pr.wait("sync", ev_stat)
        pr.wait("sync", ev_brow)
        for j in range(2):
            evA[(j, "dx")] = pr.emit("sync", lambda s, j=j: s.dma_start(
                out=pf[j % 2][:, :], in_=x_t[j, :, :]), inc="dx", inc_by=16)
        emit_stats(0)
        pr.wait("tensor", A(0, "v_x1"))
        for e_ in ev_sT_done:
            pr.wait("tensor", e_)
        pr.wait("tensor", ev_alpha)
        for e_ in ev_combine_done:
            pr.wait("tensor", e_)
        pr.wait("tensor", ev_bcast_evac)
        pr.wait("tensor", ev_borep)
        pr.wait("tensor", ev_wo)
        emit_x1t(0, ps_b)

        for i in range(NT):
            xT = xTbuf(i)
            if i + 1 < NT:
                emit_stats(i + 1)

            for (proj, wmat, rowoff, pst) in (("q", wqq, 0, ps_a),
                                              ("k", wkk, 1, ps_b)):
                pr.wait("tensor", A(i, "s_x1t"))
                if proj == "q":
                    pr.wait("tensor", A(i - 1, "v_eluq1"))
                else:
                    pr.wait("tensor", A(i - 1, "v_eluk1"))
                for half in range(2):
                    psl = pst[:, half * 512:half * 512 + 512]
                    for ic in range(NCH):
                        pr.emit("tensor", lambda t, psl=psl, wmat=wmat, ic=ic, half=half, xT=xT: t.matmul(
                            psl, xT[:, ic * P:(ic + 1) * P],
                            wmat[:, ic * D + half * 512: ic * D + half * 512 + 512],
                            start=(ic == 0), stop=False, skip_group_check=True))
                    evA[(i, f"pe_{proj}2h{half}")] = pr.emit(
                        "tensor", lambda t, psl=psl, rowoff=rowoff, half=half: t.matmul(
                            psl, ones128[:, :],
                            brep[:, rowoff * D + half * 512:
                                 rowoff * D + half * 512 + 512],
                            start=False, stop=True, skip_group_check=True),
                        inc="pe")
                    if proj == "q" and half == 0 and i > 0:
                        j = i - 1
                        pr.wait("tensor", A(j, "v_vp1"))
                        ev_tp = None
                        for icb in range(NCH):
                            ev_tp = pr.emit("tensor", lambda t, icb=icb: t.matmul(
                                ps_c[:, icb * P:(icb + 1) * P],
                                pb[2][:, icb * P:(icb + 1) * P], idn_bf[:, :],
                                start=True, stop=True,
                                skip_group_check=True), inc="pe")
                        evA[(j, "pe_vt")] = ev_tp
                        pr.wait("scalar", ev_tp)
                        evA[(j, "s_vt")] = pr.emit("scalar", lambda s, j=j: s.copy(
                            out=vTs(j)[:, :], in_=ps_c[:, :]), inc="sc")
                ebufs = (pf[4], pf[5]) if proj == "q" else (pf[6], pf[7])
                qdst = pb[2] if proj == "q" else pb[3]
                for half in range(2):
                    psl = pst[:, half * 512:half * 512 + 512]
                    pr.wait("scalar", A(i, f"pe_{proj}2h{half}"))
                    if i == 0 and half == 0:
                        pr.wait("scalar", ev_stat)
                        pr.wait("scalar", ev_brow)
                    if proj == "q" and half == 0:
                        pr.wait("scalar", A(i - 1, "pe_vt"))
                    if i == 0 and proj == "k" and half == 0:
                        pr.wait("scalar", ev_prev_w2_reader)

                    def _elu_s(s, psl=psl, ebufs=ebufs, half=half):
                        s.activation(out=ebufs[0][:, half * 512:half * 512 + 512],
                                     in_=psl, func=ACT.Relu, scale=sv["n1"][:, 0:1])
                        return s.activation(
                            out=ebufs[1][:, half * 512:half * 512 + 512],
                            in_=ebufs[0][:, half * 512:half * 512 + 512],
                            func=ACT.Exp, scale=sv["n1"][:, 0:1])
                    evA[(i, f"s_elu{proj}{half}")] = pr.emit("scalar", _elu_s,
                                                             inc="sc")
                    pr.wait("vector", A(i, f"s_elu{proj}{half}"))
                    if proj == "q" and half == 0:
                        pr.wait("vector", A(i - 1, "pe_vt"))
                        pr.wait("vector", A(i - 1, "v_vp1"))
                    if proj == "k" and half == 0:
                        pr.wait("vector", A(i - 1, "v_qks"))
                    if i == 0 and proj == "k" and half == 0:
                        pr.wait("vector", ev_prev_w2_reader)
                    _stt = lambda v, psl=psl, ebufs=ebufs, qdst=qdst, half=half: v.scalar_tensor_tensor(
                        out=qdst[:, half * 512:half * 512 + 512], in0=psl,
                        scalar=0.0,
                        in1=ebufs[1][:, half * 512:half * 512 + 512],
                        op0=ALU.max, op1=ALU.add)
                    if half == 1:
                        evA[(i, f"v_elu{proj}{half}")] = vemit(_stt)
                    else:
                        evA[(i, f"v_elu{proj}{half}")] = pr.emit("vector", _stt,
                                                                 inc="v")

            pr.wait("tensor", A(i - 1, "s_vt"))
            for half in range(2):
                psl = ps_c[:, half * 512:half * 512 + 512]
                for ic in range(NCH):
                    pr.emit("tensor", lambda t, psl=psl, ic=ic, half=half, xT=xT: t.matmul(
                        psl, xT[:, ic * P:(ic + 1) * P],
                        wvv[:, ic * D + half * 512: ic * D + half * 512 + 512],
                        start=(ic == 0), stop=False, skip_group_check=True))
                evA[(i, f"pe_v2h{half}")] = pr.emit(
                    "tensor", lambda t, psl=psl, half=half: t.matmul(
                        psl, ones128[:, :],
                        brep[:, 2 * D + half * 512: 2 * D + half * 512 + 512],
                        start=False, stop=True, skip_group_check=True), inc="pe")

            if i + 1 < NT:
                pr.wait("tensor", A(i + 1, "v_x1"))
                pr.wait("tensor", A(i, "v_eluk1"))
                emit_x1t(i + 1, ps_b)

            pr.wait("tensor", A(i, "v_eluq1"))
            pr.wait("tensor", A(i, "v_eluk1"))
            ev_smm = None
            for c in range(NCH):
                st = (i == 0 and c in (0, 4))
                ev_smm = pr.emit("tensor", lambda t, c=c, st=st, i=i: t.matmul(
                    ps_s[:, c * P:(c + 1) * P], pb[3][:, c * P:(c + 1) * P],
                    pb[2][:, c * P:(c + 1) * P], start=st,
                    stop=(i == NT - 1 and c in (3, 7)),
                    skip_group_check=True), inc="pe")
            evA[(i, "pe_smm")] = ev_smm

            if i == 0:
                pr.wait("vector", ev_bcast_mm[3])
                pr.wait("vector", ev_prev_w2_reader)
            pr.wait("vector", A(i, "pe_smm"))

            def _qks(v):
                v.tensor_tensor(out=pb[3][:, :], in0=pb[2][:, :], in1=pb[3][:, :],
                                op=ALU.mult)
                v.tensor_reduce(out=RB[:, 0:16], in_=pb[3][:, :].rearrange(
                    "p (h e) -> p h e", h=16), axis=AX.X, op=ALU.add)
                return v.tensor_copy(out=vflush[:, :], in_=ones128[:, :])
            evA[(i, "v_qks")] = pr.emit("vector", _qks, inc="v")
            pr.wait("scalar", A(i, "v_qks"))

            def _srecip(s):
                s.activation(out=RB[:, 32:48], in_=RB[:, 0:16], func=ACT.Ln,
                             bias=sv["eqk"][:, 0:1])
                s.activation(out=RB[:, 16:32], in_=RB[:, 32:48],
                             func=ACT.Exp, scale=sv["n1"][:, 0:1])
                return s.copy(out=vflush[:, :], in_=ones128[:, :])
            evA[(i, "v_recip")] = pr.emit("scalar", _srecip, inc="sc")

            for half in range(2):
                pr.wait("vector", A(i, f"pe_v2h{half}"))
                if half == 0:
                    pr.wait("vector", A(i, "v_recip"))

                def _vph(v, half=half):
                    r = None
                    for hh in range(8):
                        h = half * 8 + hh
                        r = v.tensor_scalar(
                            out=pb[2][:, h * 64:(h + 1) * 64],
                            in0=ps_c[:, h * 64:(h + 1) * 64],
                            scalar1=RB[:, 16 + h:17 + h], scalar2=None,
                            op0=ALU.mult)
                    return r
                if half == 1:
                    evA[(i, "v_vp1")] = vemit(_vph)
                else:
                    evA[(i, "v_vp0")] = pr.emit("vector", _vph, inc="v")

            if i + 2 < NT:
                pr.wait("sync", A(i + 1, "v_x1"))
                pr.wait("sync", A(i + 1, "s_sq"))
                evA[(i + 2, "dx")] = pr.emit("sync", lambda s, i=i: s.dma_start(
                    out=pf[i % 2][:, :], in_=x_t[i + 2, :, :]), inc="dx", inc_by=16)

        pr.wait("tensor", evA[(NT - 1, "v_vp1")])
        ev_tp = None
        for icb in range(NCH):
            ev_tp = pr.emit("tensor", lambda t, icb=icb: t.matmul(
                ps_c[:, icb * P:(icb + 1) * P],
                pb[2][:, icb * P:(icb + 1) * P], idn_bf[:, :],
                start=True, stop=True, skip_group_check=True), inc="pe")
        evA[(NT - 1, "pe_vt")] = ev_tp
        pr.wait("scalar", ev_tp)
        evA[(NT - 1, "s_vt")] = pr.emit("scalar", lambda s: s.copy(
            out=vTs(NT - 1)[:, :], in_=ps_c[:, :]), inc="sc")

        pr.wait("vector", evA[(NT - 1, "pe_smm")])

        def _sevac(v):
            r = None
            for h in range(16):
                c = h // 2
                off = (h % 2) * 64
                r = v.tensor_copy(
                    out=S_sb[off:off + 64, c * P + off:c * P + off + 64],
                    in_=ps_s[off:off + 64, c * P + off:c * P + off + 64])
            return r
        ev_sevac = vemit(_sevac)

        evB = {}

        def B(i, key):
            return evB.get((i, key))

        def emit_O(j):
            obank = ps_s if j % 2 == 0 else ps_b
            osb_j = pb[0] if j % 2 == 0 else pb[1]
            pr.wait("tensor", ev_sevac)
            pr.wait("tensor", A(j, "s_vt"))
            pr.wait("tensor", B(j - 2, "s_oevac"))
            if j == 1:
                pr.wait("tensor", evA[(NT - 1, "v_eluk1")])
            ev_o = None
            for c in range(NCH):
                ev_o = pr.emit("tensor", lambda t, c=c, j=j, obank=obank: t.matmul(
                    obank[:, c * P:(c + 1) * P], S_sb[:, c * P:(c + 1) * P],
                    vTs(j)[:, c * P:(c + 1) * P],
                    start=True, stop=True, skip_group_check=True), inc="pe")
            evB[(j, "pe_o")] = ev_o
            pr.wait("scalar", ev_o)
            evB[(j, "s_oevac")] = pr.emit(
                "scalar", lambda s, obank=obank, osb_j=osb_j: s.copy(
                    out=osb_j[:, :], in_=obank[:, :]), inc="sc")

        emit_O(0)
        for i in range(NT + 2):
            if i < NT:
                xb = pf[i % 2]
                abank = ps_a if i % 2 == 0 else ps_c
                yb = pf[2 + i % 2]
                sY = svY[i % 2]
                pr.wait("sync", B(i - 2, "v_xm"))
                if i <= 1:
                    pr.wait("sync", evA[(NT - 2 + i, "v_x1")])
                    pr.wait("sync", evA[(NT - 2 + i, "s_sq")])
                evB[(i, "dx")] = pr.emit("sync", lambda s, i=i, xb=xb: s.dma_start(
                    out=xb[:, :], in_=x_t[i, :, :]), inc="dx", inc_by=16)

                if i + 1 < NT:
                    emit_O(i + 1)

                pr.wait("tensor", B(i, "s_oevac"))
                pr.wait("tensor", B(i - 2, "v_y"))
                if i <= 1:
                    pr.wait("tensor", evA[(NT - 2 + i, "v_eluq1")])
                    pr.wait("tensor", evA[(NT - 2 + i, "s_eluq1")])
                if i == 1:
                    pr.wait("tensor", evA[(NT - 1, "s_vt")])
                osb = pb[0] if i % 2 == 0 else pb[1]
                for half in range(2):
                    psl = abank[:, half * 512:half * 512 + 512]
                    for ec in range(NCH):
                        pr.emit("tensor", lambda t, psl=psl, osb=osb, ec=ec, half=half: t.matmul(
                            psl, osb[:, ec * P:(ec + 1) * P],
                            wo_s[:, ec * D + half * 512: ec * D + half * 512 + 512],
                            start=(ec == 0), stop=False, skip_group_check=True))
                    evB[(i, f"pe_a{half}")] = pr.emit(
                        "tensor", lambda t, psl=psl, half=half: t.matmul(
                            psl, ones128[:, :],
                            brep[:, 3 * D + half * 512: 3 * D + half * 512 + 512],
                            start=False, stop=True, skip_group_check=True),
                        inc="pe")

                pr.wait("vector", B(i, "dx"))
                pr.wait("vector", B(i, "pe_a1"))
                pr.wait("vector", B(i - 2, "v_t2"))
                evB[(i, "v_y")] = vemit(
                    lambda v, xb=xb, abank=abank, yb=yb, i=i: v.scalar_tensor_tensor(
                        out=yb[:, :], in0=xb[:, :], scalar=rstd_all[:, i:i + 1],
                        in1=abank[:, :], op0=ALU.mult, op1=ALU.add))
                pr.wait("scalar", B(i, "v_y"))
                pr.wait("scalar", B(i - 1, "v_yvar"))
                if i == 0:
                    pr.wait("scalar", evA[(NT - 1, "v_qks")])
                pr.emit("scalar", lambda s, i=i, yb=yb, sY=sY: s.activation(
                    out=vTs(i)[:, :], in_=yb[:, :], func=ACT.Square,
                    accum_out=sY["sumsq"]))
                evB[(i, "s_ysq")] = pr.emit("scalar", lambda s, i=i, yb=yb, sY=sY: s.activation(
                    out=vTs(i)[:, :], in_=yb[:, :], func=ACT.Identity,
                    accum_out=sY["xsum"]), inc="sc")

            j = i - 1
            if 0 <= j < NT:
                xbj = pf[j % 2]
                ybj = pf[2 + j % 2]
                xmj = pf[6 + j % 2]
                sY = svY[j % 2]
                sM = svM[j % 2]
                pr.wait("vector", B(j, "s_ysq"))

                def _ysm(v, sY=sY):
                    v.tensor_scalar(out=sY["m"], in0=sY["xsum"],
                                    scalar1=1.0 / D, scalar2=None, op0=ALU.mult)
                    v.tensor_scalar(out=sY["var0"], in0=sY["sumsq"],
                                    scalar1=1.0 / D, scalar2=None, op0=ALU.mult)
                    v.tensor_scalar(out=sY["m2"], in0=sY["m"],
                                    scalar1=sY["m"], scalar2=None,
                                    op0=ALU.mult)
                    v.tensor_copy(out=sv["dmy"][:, :], in_=sv["e5"][:, :])
                    return v.tensor_tensor(out=sY["var"], in0=sY["var0"],
                                           in1=sY["m2"], op=ALU.subtract)
                evB[(j, "v_yvar")] = pr.emit("vector", _ysm, inc="v")
                pr.wait("scalar", B(j, "v_yvar"))

                def _yrstd(s, sY=sY):
                    s.activation(out=sY["lnv"], in_=sY["var"],
                                 func=ACT.Ln, bias=sv["e3"][:, 0:1])
                    return s.activation(out=sY["negmr"], in_=sY["lnv"],
                                        func=ACT.Exp, scale=sv["nh"][:, 0:1])
                evB[(j, "s_yrstd")] = pr.emit("scalar", _yrstd, inc="sc")

                pr.wait("vector", B(j, "s_yrstd"))
                evB[(j, "v_t1")] = pr.emit("vector", lambda v, ybj=ybj, sY=sY: v.scalar_tensor_tensor(
                    out=pf[4][:, :], in0=ybj[:, :], scalar=sY["m"],
                    in1=gam_b[:, :], op0=ALU.subtract, op1=ALU.mult), inc="v")
                evB[(j, "v_t2")] = pr.emit("vector", lambda v, ybj=ybj, sY=sY: v.scalar_tensor_tensor(
                    out=ybj[:, :], in0=pf[4][:, :], scalar=sY["negmr"],
                    in1=bet_b[:, :], op0=ALU.mult, op1=ALU.add), inc="v")
                pr.wait("vector", B(j - 2, "ds_xm"))
                pr.wait("vector", B(j - 2, "s_msq"))
                if j <= 1:
                    pr.wait("vector", evA[(NT - 2 + j, "v_eluk1")])
                    pr.wait("vector", evA[(NT - 2 + j, "s_eluk1")])
                evB[(j, "v_xm")] = vemit(lambda v, xmj=xmj, ybj=ybj, xbj=xbj: v.tensor_tensor(
                    out=xmj[:, :], in0=ybj[:, :], in1=xbj[:, :], op=ALU.add))
                pr.wait("scalar", B(j, "v_xm"))
                evB[(j, "ds_xm")] = pr.emit("scalar", lambda s, j=j, xmj=xmj: s.dma_start(
                    out=xm_t[j, :, :], in_=xmj[:, :]), inc="ds", inc_by=16)
                pr.wait("scalar", B(j - 1, "v_mvar"))
                pr.emit("scalar", lambda s, j=j, xmj=xmj, sM=sM: s.activation(
                    out=vTs(j)[:, :], in_=xmj[:, :], func=ACT.Square,
                    accum_out=sM["sumsq"]))
                evB[(j, "s_msq")] = pr.emit("scalar", lambda s, j=j, xmj=xmj, sM=sM: s.activation(
                    out=vTs(j)[:, :], in_=xmj[:, :], func=ACT.Identity,
                    accum_out=sM["xsum"]), inc="sc")

            k = i - 2
            if 0 <= k < NT:
                xmk = pf[6 + k % 2]
                sM = svM[k % 2]
                pr.wait("vector", B(k, "s_msq"))

                def _msm(v, sM=sM):
                    v.tensor_scalar(out=sM["m"], in0=sM["xsum"],
                                    scalar1=1.0 / D, scalar2=None, op0=ALU.mult)
                    v.tensor_scalar(out=sM["var0"], in0=sM["sumsq"],
                                    scalar1=1.0 / D, scalar2=None, op0=ALU.mult)
                    v.tensor_scalar(out=sM["m2"], in0=sM["m"],
                                    scalar1=sM["m"], scalar2=None,
                                    op0=ALU.mult)
                    v.tensor_copy(out=sv["dmy"][:, :], in_=sv["e5"][:, :])
                    return v.tensor_tensor(out=sM["var"], in0=sM["var0"],
                                           in1=sM["m2"], op=ALU.subtract)
                evB[(k, "v_mvar")] = pr.emit("vector", _msm, inc="v")
                pr.wait("scalar", B(k, "v_mvar"))

                def _mrstd(s, sM=sM):
                    s.activation(out=sM["lnv"], in_=sM["var"],
                                 func=ACT.Ln, bias=sv["e10"][:, 0:1],
                                 scale=sv["s1p"][:, 0:1])
                    return s.activation(out=sM["negmr"], in_=sM["lnv"],
                                        func=ACT.Exp, scale=sv["nh"][:, 0:1])
                evB[(k, "s_mrstd")] = pr.emit("scalar", _mrstd, inc="sc")

                pr.wait("vector", B(k, "s_mrstd"))
                pr.wait("vector", B(k, "ds_xm"))
                evB[(k, "v_t3")] = pr.emit("vector", lambda v, xmk=xmk, sM=sM: v.scalar_tensor_tensor(
                    out=xmk[:, :], in0=xmk[:, :], scalar=sM["m"],
                    in1=gamF_b[:, :], op0=ALU.subtract, op1=ALU.mult), inc="v")
                pr.wait("vector", B(k - 1, "ds_xn"))
                evB[(k, "v_xn")] = vemit(lambda v, xmk=xmk, sM=sM: v.scalar_tensor_tensor(
                    out=pf[5][:, :], in0=xmk[:, :], scalar=sM["negmr"],
                    in1=betF_b[:, :], op0=ALU.mult, op1=ALU.add))
                pr.wait("scalar", B(k, "v_xn"))
                evB[(k, "ds_xn")] = pr.emit("scalar", lambda s, k=k: s.dma_start(
                    out=xn_t[k, :, :], in_=pf[5][:, :]), inc="ds", inc_by=16)

                def _amax(v):
                    v.reduce_max(sv["amax_t"][:, :], pf[5][:, :], axis=AX.X,
                                 apply_absolute_value=True)
                    v.tensor_copy(out=sv["dmy"][:, :], in_=sv["e5"][:, :])
                    return v.tensor_tensor(out=sv["runmax"][:, :],
                                           in0=sv["runmax"][:, :],
                                           in1=sv["amax_t"][:, :], op=ALU.max)
                evB[(k, "v_amax")] = pr.emit("vector", _amax, inc="v")

        pr.wait("tensor", evB[(NT - 1, "v_amax")])
        ev_rt = pr.emit("tensor", lambda t: t.transpose(
            ps_b[0:1, 0:P], sv["runmax"][:, 0:1], idn_f[:, :]), inc="pe")
        pr.wait("vector", ev_rt)
        def _rmax(v):
            v.reduce_max(sv["gmax1"][0:1, 0:1], ps_b[0:1, 0:P], axis=AX.X)
            return v.tensor_copy(out=sv["dmy"][:, :], in_=sv["e5"][:, :])
        ev_rmax = pr.emit("vector", _rmax, inc="v")
        ev_r8 = vemit(lambda v: v.tensor_copy(
            out=red8[0:1, 0:1], in_=sv["gmax1"][0:1, 0:1]))
        pr.wait("sync", ev_r8)
        ev_d1 = pr.emit("sync", lambda s: s.dma_start(
            out=cc_in[0:1, :], in_=red8[0:1, :]), inc="dx", inc_by=16)
        pr.wait("gpsimd", ev_d1)
        ev_cc = pr.emit("gpsimd", lambda g: g.collective_compute(
            "AllReduce", ALU.max, replica_groups=[list(range(8))],
            ins=[cc_in.opt()], outs=[cc_out.opt()]), inc="cc")
        pr.wait("sync", ev_cc)
        ev_d2 = pr.emit("sync", lambda s: s.dma_start(
            out=sv["gmax1"][0:1, 0:1], in_=cc_out[0:1, 0:1]), inc="dx", inc_by=16)
        pr.wait("vector", ev_d2)
        pr.wait("vector", ev_alpha)

        def _scale_v(v):
            v.tensor_scalar(out=sv["gmaxc"][0:1, 0:1], in0=sv["gmax1"][0:1, 0:1],
                            scalar1=1e-8, scalar2=1.0 / QMAX, op0=ALU.max,
                            op1=ALU.mult)
            return v.tensor_copy(out=sv["dmy"][:, :], in_=sv["e5"][:, :])
        ev_scv = pr.emit("vector", _scale_v, inc="v")
        pr.wait("scalar", ev_scv)

        def _scale_s(s):
            s.activation(out=sv["sjk"][0:1, 0:1], in_=sv["gmaxc"][0:1, 0:1],
                         func=ACT.Ln)
            return s.activation(out=sv["invs1"][0:1, 0:1],
                                in_=sv["sjk"][0:1, 0:1], func=ACT.Exp,
                                scale=sv["n1"][0:1, 0:1])
        ev_scs = pr.emit("scalar", _scale_s, inc="sc")
        pr.wait("vector", ev_scs)

        def _scale(v):
            v.tensor_scalar(out=sv["as1"][0:1, 0:1], in0=sv["gmaxc"][0:1, 0:1],
                            scalar1=sv["alpha1"][0:1, 0:1], scalar2=None,
                            op0=ALU.mult)
            v.tensor_copy(out=pack2[0:1, 0:1], in_=sv["invs1"][0:1, 0:1])
            return v.tensor_copy(out=pack2[0:1, 1:2], in_=sv["as1"][0:1, 0:1])
        ev_sc2 = vemit(_scale)
        pr.wait("tensor", ev_sc2)
        pr.emit("tensor", lambda t: t.drain())
        ev_bc2 = pr.emit("tensor", lambda t: t.matmul(
            ps_b[:, 0:2], ones_f[0:1, :], pack2[0:1, 0:2],
            start=True, stop=True, skip_group_check=True), inc="pe")
        pr.emit("tensor", lambda t: t.drain())
        pr.wait("vector", ev_bc2)

        def _vecs(v):
            v.tensor_copy(out=sv["invs_v"][:, :], in_=ps_b[:, 0:1])
            return v.tensor_copy(out=sv["as_v"][:, :], in_=ps_b[:, 1:2])
        ev_vecs = pr.emit("vector", _vecs, inc="v")

        evC = {}

        def C(i, key):
            return evC.get((i, key))

        for i in range(NT + 1):
            if i < NT:
                xnb = pf[i % 2]
                xmb = pf[2 + i % 2]
                rq = pb[2] if i % 2 == 0 else pb[3]
                tb = ps_s if i % 2 == 0 else ps_b

                pr.wait("sync", C(i - 2, "v_rq"))
                if i <= 1:
                    pr.wait("sync", evB[(NT - 2 + i, "v_y")])
                    pr.wait("sync", evB[(NT - 2 + i, "v_xm")])
                pr.wait("sync", evB[(i, "ds_xn")])
                evC[(i, "dxn")] = pr.emit("sync", lambda s, i=i, xnb=xnb: s.dma_start(
                    out=xnb[:, :], in_=xn_t[i, :, :]), inc="dx", inc_by=16)
                pr.wait("sync", C(i - 2, "v_out"))
                if i <= 1:
                    pr.wait("sync", evB[(NT - 1, "v_xm")])
                    pr.wait("sync", evB[(NT - 1, "v_xn")])
                pr.wait("sync", evB[(i, "ds_xm")])
                evC[(i, "dxm")] = pr.emit("sync", lambda s, i=i, xmb=xmb: s.dma_start(
                    out=xmb[:, :], in_=xm_t[i, :, :]), inc="dx", inc_by=16)

                r1b = pf[6 + i % 2]
                pr.wait("scalar", C(i, "dxn"))
                pr.wait("scalar", ev_vecs)
                pr.wait("scalar", C(i - 2, "s_xqt"))
                evC[(i, "s_zp")] = pr.emit("scalar", lambda s, xnb=xnb, tb=tb: s.activation(
                    out=tb[:, :], in_=xnb[:, :], func=ACT.Identity,
                    scale=sv["invs_v"][:, 0:1], bias=sv["mgk"][:, 0:1]),
                    inc="sc")

                pr.wait("vector", C(i, "s_zp"))
                if i <= 1:
                    pr.wait("vector", evB[(NT - 2 + i, "v_xn")])
                for h in range(2):
                    pr.emit("vector", lambda v, h=h, tb=tb: v.tensor_scalar(
                        out=i32h[:, :], in0=tb[:, h * 512:h * 512 + 512],
                        scalar1=1.0, scalar2=None, op0=ALU.mult))
                    pr.emit("vector", lambda v: v.tensor_copy(
                        out=vflush[:, :], in_=ones128[:, :]))
                    pr.emit("vector", lambda v, h=h, r1b=r1b: v.tensor_copy(
                        out=r1b[:, h * 512:h * 512 + 512], in_=i32h[:, :]))

                pr.emit("vector", lambda v, tb=tb, r1b=r1b, xnb=xnb: v.tensor_tensor(
                    out=xnb[:, :], in0=tb[:, :], in1=r1b[:, :],
                    op=ALU.subtract))
                evC[(i, "v_t1")] = pr.emit(
                    "vector", lambda v, xnb=xnb, r1b=r1b: v.scalar_tensor_tensor(
                        out=xnb[:, :], in0=xnb[:, :], scalar=0.5,
                        in1=r1b[:, :], op0=ALU.is_gt, op1=ALU.add), inc="v")
                pr.wait("vector", C(i - 2, "pe_xqt"))
                if i == 0:
                    pr.wait("vector", evA[(NT - 1, "pe_vt")])
                if i == 1:
                    pr.wait("vector", evB[(NT - 1, "s_msq")])
                evC[(i, "v_rq")] = vemit(lambda v, xnb=xnb, rq=rq: v.tensor_scalar(
                    out=rq[:, :], in0=xnb[:, :], scalar1=256.0, scalar2=None,
                    op0=ALU.subtract))

            for half in range(2):
                if 0 <= i - 1 < NT:
                    fb = ps_a if (i - 1) % 2 == 0 else ps_c
                    xq = pb[0] if (i - 1) % 2 == 0 else pb[1]
                    if half == 0:
                        pr.wait("tensor", C(i - 1, "s_xqt"))
                        pr.wait("tensor", C(i - 3, "v_out"))
                    psl = fb[:, half * 512:half * 512 + 512]
                    ev_f = None
                    for ic in range(NCH):
                        ev_f = pr.emit("tensor", lambda t, psl=psl, xq=xq, ic=ic, half=half: t.matmul(
                            psl, xq[:, ic * P:(ic + 1) * P],
                            sT[:, ic * D + half * 512: ic * D + half * 512 + 512],
                            start=(ic == 0), stop=(ic == NCH - 1),
                            skip_group_check=True),
                            inc="pe" if ic == NCH - 1 else None)
                    evC[(i - 1, f"pe_f{half}")] = ev_f
                if half == 0 and i < NT:
                    pr.wait("tensor", C(i, "v_rq"))
                    pr.wait("tensor", C(i - 2, "s_xqt"))
                    if i == 1:
                        pr.wait("tensor", ev_vecs)
                    ev_tp = None
                    for icb in range(NCH):
                        ev_tp = pr.emit("tensor", lambda t, icb=icb, tb=tb, rq=rq: t.matmul(
                            tb[:, icb * P:(icb + 1) * P],
                            rq[:, icb * P:(icb + 1) * P], idn_bf[:, :],
                            start=True, stop=True, skip_group_check=True),
                            inc="pe")
                    evC[(i, "pe_xqt")] = ev_tp
                    pr.wait("scalar", ev_tp)
                    pr.wait("scalar", C(i - 2, "pe_f1"))
                    xq_i = pb[0] if i % 2 == 0 else pb[1]
                    evC[(i, "s_xqt")] = pr.emit("scalar", lambda s, tb=tb, xq_i=xq_i: s.copy(
                        out=xq_i[:, :], in_=tb[:, :]), inc="sc")

            j = i - 1
            if 0 <= j < NT:
                fb = ps_a if j % 2 == 0 else ps_c
                xmb_j = pf[2 + j % 2]
                ob = pf[4 + j % 2]
                pr.wait("vector", C(j, "pe_f1"))
                pr.wait("vector", C(j, "dxm"))
                pr.wait("vector", C(j - 2, "ds_out"))
                if j <= 1:
                    pr.wait("vector", evB[(NT - 2 + j, "ds_xn")])
                evC[(j, "v_out")] = vemit(
                    lambda v, fb=fb, xmb_j=xmb_j, ob=ob: v.scalar_tensor_tensor(
                        out=ob[:, :], in0=fb[:, :], scalar=sv["as_v"][:, 0:1],
                        in1=xmb_j[:, :], op0=ALU.mult, op1=ALU.add))
                pr.wait("scalar", C(j, "v_out"))
                evC[(j, "ds_out")] = pr.emit("scalar", lambda s, j=j, ob=ob: s.dma_start(
                    out=out_t[j, :, :], in_=ob[:, :]), inc="ds", inc_by=16)

        pr.wait("scalar", ("ds", pr.cnt["ds"]))
        pr.wait("sync", ("dx", pr.cnt["dx"]))
        pr.wait("gpsimd", ("dg", pr.cnt["dg"]))

        @block.sync
        def _(eng):
            pr.run("sync", eng, sems)

        @block.tensor
        def _(eng):
            pr.run("tensor", eng, sems)

        @block.vector
        def _(eng):
            pr.run("vector", eng, sems)

        @block.scalar
        def _(eng):
            pr.run("scalar", eng, sems)

        @block.gpsimd
        def _(eng):
            pr.run("gpsimd", eng, sems)

    return nc


_NC_CACHE = {}


def _get_nc(nt=32):
    if nt not in _NC_CACHE:
        _NC_CACHE[nt] = build_nc(nt)
    return _NC_CACHE[nt]


def kernel(L0_attn_x, L0_attn_Wq, L0_attn_bq, L0_attn_Wk, L0_attn_bk,
           L0_attn_Wv, L0_attn_bv, L0_attn_Wq2, L0_attn_bq2, L0_attn_Wk2,
           L0_attn_bk2, L0_attn_Wv2, L0_attn_bv2, L0_attn_Wo, L0_attn_bo,
           L0_attn_gamma, L0_attn_beta, L0_ffn_weight, L0_ffn_gamma,
           L0_ffn_beta):
    nc = _get_nc(32)
    x = np.ascontiguousarray(np.asarray(L0_attn_x, dtype=np.float32))
    w = {
        "wq": L0_attn_Wq, "wk": L0_attn_Wk, "wv": L0_attn_Wv,
        "wq2": L0_attn_Wq2, "wk2": L0_attn_Wk2, "wv2": L0_attn_Wv2,
        "wo": L0_attn_Wo, "fw": L0_ffn_weight,
        "bq": L0_attn_bq, "bk": L0_attn_bk, "bv": L0_attn_bv,
        "bq2": L0_attn_bq2, "bk2": L0_attn_bk2, "bv2": L0_attn_bv2,
        "bo": L0_attn_bo, "gam": L0_attn_gamma, "bet": L0_attn_beta,
        "gamF": L0_ffn_gamma, "betF": L0_ffn_beta,
    }
    w = {k: np.ascontiguousarray(np.asarray(v, dtype=np.float32))
         for k, v in w.items()}
    in_maps = []
    for c in range(8):
        m = {"x": np.ascontiguousarray(x[c])}
        m.update(w)
        in_maps.append(m)
    res = run_bass_kernel_spmd(nc, in_maps, core_ids=list(range(8)))
    out = np.stack([np.asarray(res.results[c]["out"]) for c in range(8)], axis=0)
    return out.astype(np.float32)


# version 22
# speedup vs baseline: 1.0807x; 1.0807x over previous
import numpy as np
from contextlib import ExitStack

import concourse.bass as bass
import concourse.mybir as mybir
from concourse.bass_utils import run_bass_kernel_spmd

F32 = mybir.dt.float32
BF16 = mybir.dt.bfloat16
I32 = mybir.dt.int32
ALU = mybir.AluOpType
ACT = mybir.ActivationFunctionType
AX = mybir.AxisListType

P = 128
D = 1024
NCH = D // P
MAGIC = 12582912.0
EPS_LN1 = 1e-5
EPS_ATT = 1e-3
EPS_QK = 1e-6
EPS_F = 1e-5
QMAX = 127.0
THRESH = 0.7

ENGS = ("tensor", "vector", "scalar", "gpsimd", "sync")
_LAST_PROG = None


class Prog:

    def __init__(self):
        self.ops = {e: [] for e in ENGS}
        self.cnt = {}

    def emit(self, eng, fn, inc=None, inc_by=1):
        if inc is not None:
            self.cnt[inc] = self.cnt.get(inc, 0) + inc_by
            ev = (inc, self.cnt[inc])
        else:
            ev = None
        self.ops[eng].append(("OP", fn, inc, inc_by))
        return ev

    def wait(self, eng, ev):
        if ev is None:
            return
        sem, val = ev
        self.ops[eng].append(("WAIT", sem, val, None))

    def run(self, eng_name, eng, sems):
        for entry in self.ops[eng_name]:
            if entry[0] == "WAIT":
                eng.wait_ge(sems[entry[1]], entry[2])
            else:
                _, fn, inc, inc_by = entry
                inst = fn(eng)
                if inc is not None:
                    inst.then_inc(sems[inc], inc_by)


def build_nc(NT=32):
    L = NT * P
    nc = bass.Bass("TRN2", target_bir_lowering=False,
                   detect_race_conditions=False)

    x_d = nc.dram_tensor("x", [L, D], F32, kind="ExternalInput").ap()
    wdr = {}
    for nm in ("wq", "wk", "wv", "wq2", "wk2", "wv2", "wo", "fw"):
        wdr[nm] = nc.dram_tensor(nm, [D, D], F32, kind="ExternalInput").ap()
    vdr = {}
    for nm in ("bq", "bk", "bv", "bq2", "bk2", "bv2", "bo", "gam", "bet",
               "gamF", "betF"):
        vdr[nm] = nc.dram_tensor(nm, [D], F32, kind="ExternalInput").ap()
    out_d = nc.dram_tensor("out", [L, D], F32, kind="ExternalOutput").ap()
    sp_xn = nc.dram_tensor("sp_xn", [L, D], F32).ap()
    sp_xm = nc.dram_tensor("sp_xm", [L, D], F32).ap()
    cc_in = nc.dram_tensor("cc_in", [1, 8], F32).ap()
    cc_out = nc.dram_tensor("cc_out", [1, 8], F32).ap()

    x_t = x_d.rearrange("(n p) d -> n p d", p=P)
    out_t = out_d.rearrange("(n p) d -> n p d", p=P)
    xn_t = sp_xn.rearrange("(n p) d -> n p d", p=P)
    xm_t = sp_xm.rearrange("(n p) d -> n p d", p=P)

    pr = Prog()
    global _LAST_PROG
    _LAST_PROG = pr

    with ExitStack() as ctx:
        E = ctx.enter_context

        wqq = E(nc.sbuf_tensor("wqq", [P, NCH * D], BF16))
        wkk = E(nc.sbuf_tensor("wkk", [P, NCH * D], BF16))
        wvv = E(nc.sbuf_tensor("wvv", [P, NCH * D], BF16))
        wo_s = E(nc.sbuf_tensor("wo_s", [P, NCH * D], BF16))
        sT = E(nc.sbuf_tensor("sT", [P, NCH * D], BF16))
        vT = E(nc.sbuf_tensor("vT", [P, max(NT - 1, 18) * D], BF16))
        gam_b = E(nc.sbuf_tensor("gam_b", [P, D], BF16))
        bet_b = E(nc.sbuf_tensor("bet_b", [P, D], BF16))
        gamF_b = E(nc.sbuf_tensor("gamF_b", [P, D], BF16))
        betF_b = E(nc.sbuf_tensor("betF_b", [P, D], BF16))
        pf = [E(nc.sbuf_tensor(f"pf{j}", [P, D], F32)) for j in range(8)]
        pb = [E(nc.sbuf_tensor(f"pb{j}", [P, D], BF16)) for j in range(5)]
        i32h = E(nc.sbuf_tensor("i32h", [P, 512], I32))
        S_sb = E(nc.sbuf_tensor("S_sb", [P, D], BF16))
        brep = E(nc.sbuf_tensor("brep", [P, 4 * D], BF16))
        ones128 = E(nc.sbuf_tensor("ones128", [P, P], BF16))
        ones_f = E(nc.sbuf_tensor("ones_f", [1, P], F32))
        idn_bf = E(nc.sbuf_tensor("idn_bf", [P, P], BF16))
        idn_f = E(nc.sbuf_tensor("idn_f", [P, P], F32))
        bq_bf = E(nc.sbuf_tensor("bq_bf", [P, 3 * NCH], BF16))
        bqb = pb[3]
        vflush = E(nc.sbuf_tensor("vflush", [P, P], F32))
        svYt = E(nc.sbuf_tensor("svYt", [P, 16], F32))
        svMt = E(nc.sbuf_tensor("svMt", [P, 16], F32))
        svY = [{}, {}]
        svM = [{}, {}]
        _svnames = ("xsum", "sumsq", "m", "m2", "var0", "var", "lnv", "negmr")
        for par in range(2):
            for ix, nm_ in enumerate(_svnames):
                svY[par][nm_] = svYt[:, par * 8 + ix: par * 8 + ix + 1]
                svM[par][nm_] = svMt[:, par * 8 + ix: par * 8 + ix + 1]
        rstd_all = E(nc.sbuf_tensor("rstd_all", [P, NT], F32))
        nm_all = E(nc.sbuf_tensor("nm_all", [P, NT], F32))
        sv = {}
        for nm_ in ("xsum", "m", "m2", "var", "var0", "dmy", "lnv", "negmr", "sumsq",
                    "wabs", "asum", "cnt", "delta_v", "ndelta_v", "tpart",
                    "alpha1", "gmax1", "gmaxc", "invs1", "as1",
                    "runmax", "amax_t", "invs_v", "as_v", "e5", "e3", "e10",
                    "s1p", "nh", "n1", "sjk", "eqk", "mgk"):
            sv[nm_] = E(nc.sbuf_tensor("sv_" + nm_, [P, 1], F32))
        red8 = E(nc.sbuf_tensor("sv_red8", [1, 8], F32))
        pack2 = E(nc.sbuf_tensor("sv_pack2", [1, 2], F32))

        ps_s = E(nc.psum_tensor("ps_s", [P, D], F32))
        ps_a = E(nc.psum_tensor("ps_a", [P, D], F32))
        ps_b = E(nc.psum_tensor("ps_b", [P, D], F32))
        ps_c = E(nc.psum_tensor("ps_c", [P, D], F32))
        psums = [ps_s, ps_a, ps_b, ps_c]

        sem_names = ["dx", "ds", "dg", "pe", "v", "sc", "g", "cc"]
        sems = {n: E(nc.semaphore("sem_" + n)) for n in sem_names}
        block = E(nc.Block())

        def vTs(j):
            return pb[3][:, :] if j == NT - 1 else vT[:, j * D:(j + 1) * D]

        WT_scr = vT[:, 0:8192]
        W2_scr = vT[:, 8192:16384]
        wcbuf = vT[:, 16384:17408]

        def bfv(ps, bank):
            return ps[:, bank * 512:(bank + 1) * 512].bitcast(BF16)

        def vemit(fn, inc="v"):
            pr.emit("vector", fn)
            return pr.emit("vector", lambda v: v.tensor_copy(
                out=vflush[:, :], in_=ones128[:, :]), inc=inc)

        ev_iota = pr.emit("gpsimd", lambda g: g.iota(
            vflush[:, :], pattern=[[-1, P]], base=0, channel_multiplier=1,
            allow_small_or_imprecise_dtypes=True), inc="g")


        pr.wait("vector", ev_iota)

        def _consts(v):
            v.memset(ones128[:, :], 1.0)
            v.memset(ones_f[0:1, :], 1.0)
            v.memset(sv["e5"][:, :], EPS_LN1)
            v.memset(sv["e3"][:, :], EPS_ATT)
            v.memset(sv["e10"][:, :], EPS_LN1 * EPS_F)
            v.memset(sv["s1p"][:, :], 1.0 + EPS_F)
            v.memset(sv["nh"][:, :], -0.5)
            v.memset(sv["n1"][:, :], -1.0)
            v.memset(sv["eqk"][:, :], EPS_QK)
            v.memset(sv["mgk"][:, :], 256.0)
            v.memset(sv["wabs"][:, :], 0.0)
            v.memset(sv["asum"][:, :], 0.0)
            v.memset(sv["cnt"][:, :], 0.0)
            v.memset(S_sb[:, :], 0.0)
            v.memset(sv["runmax"][:, :], 0.0)
            v.memset(red8[0:1, :], 0.0)
            v.tensor_scalar(out=idn_f[:, :], in0=vflush[:, :], scalar1=0.0,
                            scalar2=None, op0=ALU.is_equal)
            v.tensor_scalar(out=idn_bf[:, :], in0=vflush[:, :], scalar1=0.0,
                            scalar2=None, op0=ALU.is_equal)
            return v.tensor_copy(out=sv["dmy"][:, :], in_=sv["e5"][:, :])
        ev_consts = vemit(_consts)
        ev_idn = ev_consts

        pr.wait("gpsimd", ev_consts)
        ev_ccz = pr.emit("gpsimd", lambda g: g.dma_start(
            out=cc_in[0:1, :], in_=red8[0:1, :]), inc="dg", inc_by=16)
        pr.wait("gpsimd", ev_ccz)
        ev_ccw = pr.emit("gpsimd", lambda g: g.collective_compute(
            "AllReduce", ALU.max, replica_groups=[list(range(8))],
            ins=[cc_in.opt()], outs=[cc_out.opt()]), inc="cc")

        ev_rows = None
        for j, nm in enumerate(("gam", "bet", "gamF", "betF")):
            ev_rows = pr.emit("sync", lambda s, nm=nm, j=j: s.dma_start(
                out=pf[j][0:1, :], in_=vdr[nm][None, :]), inc="dx", inc_by=16)

        ev_bo_row = pr.emit("sync", lambda s: s.dma_start(
            out=pf[6][0:1, :], in_=vdr["bo"][None, :]), inc="dx", inc_by=16)
        ev_bsm = None
        for j, nm in enumerate(("bq", "bk", "bv")):
            ev_bsm = pr.emit("gpsimd", lambda g, nm=nm, j=j: g.dma_start(
                out=bq_bf[:, j * NCH:(j + 1) * NCH],
                in_=vdr[nm].rearrange("(c p) -> p c", p=P),
                allow_slow_non_contiguous=True), inc="dg", inc_by=16)
        ev_wo = pr.emit("gpsimd", lambda g: g.dma_start(
            out=wo_s[:, :].rearrange("p (c j) -> p c j", c=NCH),
            in_=wdr["wo"].rearrange("(c p) j -> p c j", p=P)), inc="dg", inc_by=16)

        pr.wait("tensor", ev_consts)
        pr.wait("tensor", ev_idn)
        pr.wait("tensor", ev_rows)
        bdsts = [gam_b, bet_b, gamF_b, betF_b]
        ev_bcast_mm = {}
        ev_bc = None
        for j in range(4):
            pr.wait("tensor", ev_bc)
            pr.emit("tensor", lambda t: t.drain())
            ev_mm = None
            for half in range(2):
                ev_mm = pr.emit("tensor", lambda t, j=j, half=half: t.matmul(
                    ps_s[:, half * 512:(half + 1) * 512], ones_f[0:1, :],
                    pf[j][0:1, half * 512:(half + 1) * 512],
                    start=True, stop=True, skip_group_check=True), inc="pe")
            pr.emit("tensor", lambda t: t.drain())
            ev_bcast_mm[j] = ev_mm
            pr.wait("vector", ev_mm)
            ev_bc = pr.emit("vector", lambda v, j=j: v.tensor_copy(
                out=bdsts[j][:, :], in_=ps_s[:, :]), inc="v")
        ev_bcast_evac = ev_bc

        pr.wait("tensor", ev_bo_row)
        pr.emit("tensor", lambda t: t.drain())
        ev_bom = None
        for half in range(2):
            ev_bom = pr.emit("tensor", lambda t, half=half: t.matmul(
                ps_a[:, half * 512:(half + 1) * 512], ones_f[0:1, :],
                pf[6][0:1, half * 512:(half + 1) * 512],
                start=True, stop=True, skip_group_check=True), inc="pe")
        pr.emit("tensor", lambda t: t.drain())
        pr.wait("vector", ev_bom)
        ev_borep = vemit(lambda v: v.tensor_scalar(
            out=brep[:, 3 * D:4 * D], in0=ps_a[:, :], scalar1=1.0 / P,
            scalar2=None, op0=ALU.mult))

        ev_combine_done = []
        ev_prev_w2_reader = None
        ev_prev_b2m = None
        ev_brow = None
        for widx, (w1n, w2n, wdst) in enumerate(
                (("wq", "wq2", wqq), ("wk", "wk2", wkk), ("wv", "wv2", wvv))):
            pr.wait("gpsimd", ev_prev_w2_reader)
            ev_w2 = pr.emit("gpsimd", lambda g, w2n=w2n: g.dma_start(
                out=W2_scr.rearrange("p (c j) -> p c j", c=NCH),
                in_=wdr[w2n].rearrange("(c p) j -> p c j", p=P)),
                inc="dg", inc_by=16)
            prev_evac = [ev_prev_w2_reader, ev_prev_w2_reader]
            prev_tp_last = None
            for ic in range(NCH):
                pr.wait("gpsimd", prev_tp_last)
                ev_chunk = pr.emit("gpsimd", lambda g, w1n=w1n, ic=ic: g.dma_start(
                    out=wcbuf.rearrange("p (o j) -> p o j", o=1),
                    in_=wdr[w1n][ic * P:(ic + 1) * P, :].rearrange(
                        "p (o j) -> p o j", o=1)), inc="dg", inc_by=16)
                pr.wait("tensor", ev_chunk)
                for batch in range(2):
                    pr.wait("tensor", prev_evac[batch])
                    ev_tp = None
                    for jj in range(4):
                        jc = batch * 4 + jj
                        ev_tp = pr.emit("tensor", lambda t, jc=jc, jj=jj, batch=batch: t.matmul(
                            ps_c[:, batch * 512 + jj * P:batch * 512 + (jj + 1) * P],
                            wcbuf[:, jc * P:(jc + 1) * P], idn_bf[:, :],
                            start=True, stop=True, skip_group_check=True), inc="pe")
                    prev_tp_last = ev_tp
                    pr.wait("scalar", ev_tp)
                    ev_ev = pr.emit("scalar", lambda s, ic=ic, batch=batch: s.copy(
                        out=WT_scr.rearrange("p (j i) -> p j i", j=NCH)[
                            :, batch * 4:(batch + 1) * 4, ic * P:(ic + 1) * P],
                        in_=ps_c[:, batch * 512:batch * 512 + 512].rearrange(
                            "p (j i) -> p j i", j=4)), inc="sc")
                    prev_evac[batch] = ev_ev
            pr.wait("tensor", ev_w2)
            pr.wait("tensor", prev_evac[0])
            pr.wait("tensor", prev_evac[1])
            ev_evac_prev = {}
            ev_last_combine_mm = None
            for rnd in range(2):
                evs_last = {}
                for jc in range(NCH):
                    for ibl in range(4):
                        ib = rnd * 4 + ibl
                        for half in range(2):
                            slot = ibl * 2 + half
                            psl = psums[slot // 2][
                                :, (slot % 2) * 512:(slot % 2) * 512 + 512]
                            if jc == 0:
                                pr.wait("tensor", ev_evac_prev.get(slot))
                            ev_mm = pr.emit("tensor", lambda t, psl=psl, jc=jc, ib=ib, half=half, wdst=wdst: t.matmul(
                                psl,
                                WT_scr[:, jc * D + ib * P: jc * D + (ib + 1) * P],
                                W2_scr[:, jc * D + half * 512: jc * D + half * 512 + 512],
                                start=(jc == 0), stop=(jc == NCH - 1),
                                skip_group_check=True), inc="pe")
                            evs_last[slot] = ev_mm
                ev_last_combine_mm = ev_mm
                for ibl in range(4):
                    ib = rnd * 4 + ibl
                    for half in range(2):
                        slot = ibl * 2 + half
                        psl = psums[slot // 2][
                            :, (slot % 2) * 512:(slot % 2) * 512 + 512]
                        pr.wait("vector", evs_last[slot])
                        ev_e = pr.emit("vector", lambda v, psl=psl, wdst=wdst, ib=ib, half=half: v.tensor_copy(
                            out=wdst[:, ib * D + half * 512: ib * D + half * 512 + 512],
                            in_=psl), inc="v")
                        ev_evac_prev[slot] = ev_e
                        ev_combine_done.append(ev_e)
            pr.wait("vector", ev_prev_w2_reader)
            pr.wait("vector", ev_bsm)
            ev_bqb = vemit(lambda v, widx=widx: v.tensor_copy(
                out=bqb[:, :].rearrange("p (c m) -> p m c", c=NCH),
                in_=bq_bf[:, widx * NCH:(widx + 1) * NCH].unsqueeze(1).broadcast_to(
                    [P, P, NCH])))
            b2n = ("bq2", "bk2", "bv2")[widx]
            pr.wait("sync", ev_prev_b2m)
            ev_b2 = pr.emit("sync", lambda s, b2n=b2n: s.dma_start(
                out=pf[5][0:1, :], in_=vdr[b2n][None, :]), inc="dx", inc_by=16)
            pr.wait("tensor", ev_b2)
            pr.wait("tensor", ev_evac_prev.get(2))
            pr.wait("tensor", ev_evac_prev.get(3))
            if widx == 0:
                pr.wait("tensor", ev_borep)
            pr.emit("tensor", lambda t: t.drain())
            ev_b2m = None
            for half in range(2):
                ev_b2m = pr.emit("tensor", lambda t, half=half: t.matmul(
                    ps_a[:, half * 512:(half + 1) * 512], ones_f[0:1, :],
                    pf[5][0:1, half * 512:(half + 1) * 512],
                    start=True, stop=True, skip_group_check=True), inc="pe")
            pr.emit("tensor", lambda t: t.drain())
            ev_prev_b2m = ev_b2m
            pr.wait("vector", ev_b2m)
            ev_b2r = vemit(lambda v: v.tensor_scalar(
                out=pf[6][:, :], in0=ps_a[:, :], scalar1=1.0 / P,
                scalar2=None, op0=ALU.mult))
            pr.wait("tensor", ev_bqb)
            pr.wait("tensor", ev_evac_prev[0])
            pr.wait("tensor", ev_evac_prev[1])
            evb = None
            for jc in range(NCH):
                for half in range(2):
                    psl = ps_s[:, half * 512:half * 512 + 512]
                    evb = pr.emit("tensor", lambda t, psl=psl, jc=jc, half=half: t.matmul(
                        psl, bqb[:, jc * P:(jc + 1) * P],
                        W2_scr[:, jc * D + half * 512: jc * D + half * 512 + 512],
                        start=(jc == 0), stop=(jc == NCH - 1),
                        skip_group_check=True), inc="pe" if jc == NCH - 1 else None)
            ev_prev_w2_reader = evb
            pr.wait("vector", evb)
            pr.wait("vector", ev_b2r)

            def _bias_fin(v, widx=widx):
                return v.scalar_tensor_tensor(
                    out=brep[:, widx * D:(widx + 1) * D], in0=ps_s[:, :],
                    scalar=1.0 / P, in1=pf[6][:, :], op0=ALU.mult, op1=ALU.add)
            ev_brow = vemit(_bias_fin)
            ev_combine_done.append(ev_brow)
            pr.wait("tensor", ev_brow)

        ev_p1 = {}
        ev_red = None
        for och in range(NCH):
            cc = och % 2
            buf = pf[6 + cc]
            pr.wait("sync", ev_p1.get(cc))
            if och < 2:
                pr.wait("sync", ev_brow)
            ev_l = pr.emit("sync", lambda s, och=och, buf=buf: s.dma_start(
                out=buf[:, :], in_=wdr["fw"][och * P:(och + 1) * P, :]),
                inc="dx", inc_by=16)
            pr.wait("vector", ev_l)

            def _red(v, buf=buf):
                v.reduce_sum(sv["m2"][:, :], buf[:, :], axis=AX.X,
                             apply_absolute_value=True)
                v.tensor_copy(out=sv["dmy"][:, :], in_=sv["e5"][:, :])
                return v.tensor_tensor(out=sv["wabs"][:, :], in0=sv["wabs"][:, :],
                                       in1=sv["m2"][:, :], op=ALU.add)
            ev_red = pr.emit("vector", _red, inc="v")
            ev_p1[cc] = ev_red
        pr.wait("tensor", ev_red)
        pr.wait("tensor", ev_brow)
        ev_tp = pr.emit("tensor", lambda t: t.transpose(
            ps_b[0:1, 0:P], sv["wabs"][:, 0:1], idn_f[:, :]), inc="pe")
        pr.wait("vector", ev_tp)

        def _delta(v):
            v.reduce_sum(sv["tpart"][0:1, 0:1], ps_b[0:1, 0:P], axis=AX.X)
            v.tensor_copy(out=sv["dmy"][:, :], in_=sv["e5"][:, :])
            return v.tensor_scalar(out=sv["tpart"][0:1, 0:1],
                                   in0=sv["tpart"][0:1, 0:1],
                                   scalar1=THRESH / (D * D), scalar2=None,
                                   op0=ALU.mult)
        ev_dl = vemit(_delta)
        pr.wait("tensor", ev_dl)
        pr.emit("tensor", lambda t: t.drain())
        ev_dbc = pr.emit("tensor", lambda t: t.matmul(
            ps_b[:, 0:1], ones_f[0:1, :], sv["tpart"][0:1, 0:1],
            start=True, stop=True, skip_group_check=True), inc="pe")
        pr.emit("tensor", lambda t: t.drain())
        pr.wait("vector", ev_dbc)

        def _dvec(v):
            v.tensor_copy(out=sv["delta_v"][:, :], in_=ps_b[:, 0:1])
            v.tensor_copy(out=sv["dmy"][:, :], in_=sv["e5"][:, :])
            return v.tensor_scalar(out=sv["ndelta_v"][:, :], in0=sv["delta_v"][:, :],
                                   scalar1=-1.0, scalar2=None, op0=ALU.mult)
        ev_dv = pr.emit("vector", _dvec, inc="v")

        prev_sT_evac = [None, None]
        prev_s_tp = None
        ev_stat = None
        ev_sgn = None
        for och in range(NCH):
            cc = och % 2
            buf = pf[6 + cc]
            pr.wait("sync", ev_stat if och >= 2 else ev_red)
            ev_l = pr.emit("sync", lambda s, och=och, buf=buf: s.dma_start(
                out=buf[:, :], in_=wdr["fw"][och * P:(och + 1) * P, :]),
                inc="dx", inc_by=16)
            pr.wait("vector", ev_l)
            pr.wait("vector", ev_dv)
            pr.wait("vector", prev_s_tp)

            def _sgn(v, buf=buf):
                v.tensor_scalar(out=pf[5][:, :], in0=buf[:, :],
                                scalar1=sv["delta_v"][:, 0:1], scalar2=None,
                                op0=ALU.is_gt)
                v.tensor_scalar(out=pf[0][:, :], in0=buf[:, :],
                                scalar1=sv["ndelta_v"][:, 0:1], scalar2=None,
                                op0=ALU.is_lt)
                return v.tensor_tensor(out=pb[0][:, :], in0=pf[5][:, :],
                                       in1=pf[0][:, :], op=ALU.subtract)
            ev_sgn = vemit(_sgn)

            def _stats(v, buf=buf):
                v.tensor_tensor(out=pf[5][:, :], in0=buf[:, :], in1=pb[0][:, :],
                                op=ALU.mult)
                v.reduce_sum(sv["m2"][:, :], pf[5][:, :], axis=AX.X)
                v.reduce_sum(sv["var0"][:, :], pb[0][:, :], axis=AX.X,
                             apply_absolute_value=True)
                v.tensor_tensor(out=sv["asum"][:, :], in0=sv["asum"][:, :],
                                in1=sv["m2"][:, :], op=ALU.add)
                return v.tensor_tensor(out=sv["cnt"][:, :], in0=sv["cnt"][:, :],
                                       in1=sv["var0"][:, :], op=ALU.add)
            ev_stat = pr.emit("vector", _stats, inc="v")
            pr.wait("tensor", ev_sgn)
            for batch in range(2):
                pr.wait("tensor", prev_sT_evac[batch])
                ev_tp = None
                for jj in range(4):
                    icb = batch * 4 + jj
                    ev_tp = pr.emit("tensor", lambda t, icb=icb, jj=jj, batch=batch: t.matmul(
                        ps_c[:, batch * 512 + jj * P:batch * 512 + (jj + 1) * P],
                        pb[0][:, icb * P:(icb + 1) * P], idn_bf[:, :],
                        start=True, stop=True, skip_group_check=True), inc="pe")
                prev_s_tp = ev_tp
                pr.wait("scalar", ev_tp)
                ev_ev = pr.emit("scalar", lambda s, och=och, batch=batch: s.copy(
                    out=sT.rearrange("p (i o) -> p i o", i=NCH)[
                        :, batch * 4:(batch + 1) * 4, och * P:(och + 1) * P],
                    in_=ps_c[:, batch * 512:batch * 512 + 512].rearrange(
                        "p (i o) -> p i o", i=4)), inc="sc")
                prev_sT_evac[batch] = ev_ev
        ev_sT_done = list(prev_sT_evac)
        ev_sT_tp_last = prev_s_tp

        pr.wait("tensor", ev_stat)
        pr.emit("tensor", lambda t: t.transpose(
            ps_b[0:1, 0:P], sv["asum"][:, 0:1], idn_f[:, :]), inc="pe")
        ev_tp3 = pr.emit("tensor", lambda t: t.transpose(
            ps_b[0:1, 128:256], sv["cnt"][:, 0:1], idn_f[:, :]), inc="pe")
        pr.wait("vector", ev_tp3)

        def _alpha_v(v):
            v.reduce_sum(sv["m2"][0:1, 0:1], ps_b[0:1, 128:256], axis=AX.X)
            v.reduce_sum(sv["alpha1"][0:1, 0:1], ps_b[0:1, 0:P], axis=AX.X)
            v.tensor_scalar(out=sv["m2"][0:1, 0:1], in0=sv["m2"][0:1, 0:1],
                            scalar1=1.0, scalar2=None, op0=ALU.max)
            return v.tensor_copy(out=sv["dmy"][:, :], in_=sv["e5"][:, :])
        ev_alphav = pr.emit("vector", _alpha_v, inc="v")
        pr.wait("scalar", ev_alphav)

        def _alpha_s(s):
            s.activation(out=sv["sjk"][0:1, 0:1], in_=sv["m2"][0:1, 0:1],
                         func=ACT.Ln)
            return s.activation(out=sv["m2"][0:1, 0:1], in_=sv["sjk"][0:1, 0:1],
                                func=ACT.Exp, scale=sv["n1"][0:1, 0:1])
        ev_alphas = pr.emit("scalar", _alpha_s, inc="sc")
        pr.wait("vector", ev_alphas)
        ev_alpha = vemit(lambda v: v.tensor_scalar(
            out=sv["alpha1"][0:1, 0:1], in0=sv["alpha1"][0:1, 0:1],
            scalar1=sv["m2"][0:1, 0:1], scalar2=None, op0=ALU.mult))

        RB = pf[3]
        evA = {}

        def A(i, key):
            return evA.get((i, key))

        def xTbuf(j):
            return pb[1] if j % 2 == 0 else pb[4]

        def emit_stats(j, dest_ev_prefix="A"):
            xb = pf[j % 2]
            pr.wait("scalar", A(j, "dx"))
            pr.wait("scalar", A(j - 1, "v_var"))
            pr.emit("scalar", lambda s, xb=xb, j=j: s.activation(
                out=vT[:, min(j, NT - 2) * D:min(j, NT - 2) * D + D],
                in_=xb[:, :], func=ACT.Square,
                accum_out=sv["sumsq"][:, 0:1]))
            evA[(j, "s_sq")] = pr.emit("scalar", lambda s, xb=xb, j=j: s.activation(
                out=vT[:, min(j, NT - 2) * D:min(j, NT - 2) * D + D],
                in_=xb[:, :], func=ACT.Identity,
                accum_out=sv["xsum"][:, 0:1]), inc="sc")
            pr.wait("vector", A(j, "s_sq"))

            def _smalls(v):
                v.tensor_scalar(out=sv["m"][:, :], in0=sv["xsum"][:, :],
                                scalar1=1.0 / D, scalar2=None, op0=ALU.mult)
                v.tensor_scalar(out=sv["var0"][:, :], in0=sv["sumsq"][:, :],
                                scalar1=1.0 / D, scalar2=None, op0=ALU.mult)
                v.tensor_scalar(out=sv["m2"][:, :], in0=sv["m"][:, :],
                                scalar1=sv["m"][:, 0:1], scalar2=None, op0=ALU.mult)
                v.tensor_copy(out=sv["dmy"][:, :], in_=sv["e5"][:, :])
                return v.tensor_tensor(out=sv["var"][:, :], in0=sv["var0"][:, :],
                                       in1=sv["m2"][:, :], op=ALU.subtract)
            evA[(j, "v_var")] = pr.emit("vector", _smalls, inc="v")
            pr.wait("scalar", A(j, "v_var"))

            def _rstd(s, j=j):
                s.activation(out=sv["lnv"][:, :], in_=sv["var"][:, :], func=ACT.Ln,
                             bias=sv["e5"][:, 0:1])
                return s.activation(out=rstd_all[:, j:j + 1], in_=sv["lnv"][:, :],
                                    func=ACT.Exp, scale=sv["nh"][:, 0:1])
            evA[(j, "s_rstd")] = pr.emit("scalar", _rstd, inc="sc")
            pr.wait("vector", A(j, "s_rstd"))
            pr.wait("vector", A(j - 1, "pe_x1t"))
            if j == 0:
                pr.wait("vector", ev_sT_tp_last)

            def _x1(v, j=j, xb=xb):
                v.tensor_scalar(out=nm_all[:, j:j + 1], in0=sv["m"][:, :],
                                scalar1=rstd_all[:, j:j + 1], scalar2=-1.0,
                                op0=ALU.mult, op1=ALU.mult)
                v.tensor_copy(out=sv["dmy"][:, :], in_=sv["e5"][:, :])
                return v.tensor_scalar(out=pb[0][:, :], in0=xb[:, :],
                                       scalar1=rstd_all[:, j:j + 1],
                                       scalar2=nm_all[:, j:j + 1],
                                       op0=ALU.mult, op1=ALU.add)
            evA[(j, "v_x1")] = vemit(_x1)

        def emit_x1t(j, bank_ps):
            ev_tp = None
            for icb in range(NCH):
                ev_tp = pr.emit("tensor", lambda t, icb=icb, bank_ps=bank_ps: t.matmul(
                    bank_ps[:, icb * P:(icb + 1) * P],
                    pb[0][:, icb * P:(icb + 1) * P], idn_bf[:, :],
                    start=True, stop=True, skip_group_check=True), inc="pe")
            evA[(j, "pe_x1t")] = ev_tp
            pr.wait("vector", ev_tp)
            evA[(j, "s_x1t")] = vemit(lambda v, j=j, bank_ps=bank_ps: v.tensor_copy(
                out=xTbuf(j)[:, :], in_=bank_ps[:, :]))

        pr.wait("sync", ev_stat)
        pr.wait("sync", ev_brow)
        for j in range(2):
            evA[(j, "dx")] = pr.emit("sync", lambda s, j=j: s.dma_start(
                out=pf[j % 2][:, :], in_=x_t[j, :, :]), inc="dx", inc_by=16)
        emit_stats(0)
        pr.wait("tensor", A(0, "v_x1"))
        for e_ in ev_sT_done:
            pr.wait("tensor", e_)
        pr.wait("tensor", ev_alpha)
        for e_ in ev_combine_done:
            pr.wait("tensor", e_)
        pr.wait("tensor", ev_bcast_evac)
        pr.wait("tensor", ev_borep)
        pr.wait("tensor", ev_wo)
        emit_x1t(0, ps_b)

        for i in range(NT):
            xT = xTbuf(i)
            if i + 1 < NT:
                emit_stats(i + 1)

            for (proj, wmat, rowoff, pst) in (("q", wqq, 0, ps_a),
                                              ("k", wkk, 1, ps_b)):
                pr.wait("tensor", A(i, "s_x1t"))
                if proj == "q":
                    pr.wait("tensor", A(i - 1, "v_eluq1"))
                else:
                    pr.wait("tensor", A(i - 1, "v_eluk1"))
                for half in range(2):
                    psl = pst[:, half * 512:half * 512 + 512]
                    for ic in range(NCH):
                        pr.emit("tensor", lambda t, psl=psl, wmat=wmat, ic=ic, half=half, xT=xT: t.matmul(
                            psl, xT[:, ic * P:(ic + 1) * P],
                            wmat[:, ic * D + half * 512: ic * D + half * 512 + 512],
                            start=(ic == 0), stop=False, skip_group_check=True))
                    evA[(i, f"pe_{proj}2h{half}")] = pr.emit(
                        "tensor", lambda t, psl=psl, rowoff=rowoff, half=half: t.matmul(
                            psl, ones128[:, :],
                            brep[:, rowoff * D + half * 512:
                                 rowoff * D + half * 512 + 512],
                            start=False, stop=True, skip_group_check=True),
                        inc="pe")
                    if proj == "q" and half == 0 and i > 0:
                        j = i - 1
                        pr.wait("tensor", A(j, "v_vp1"))
                        ev_tp = None
                        for icb in range(NCH):
                            ev_tp = pr.emit("tensor", lambda t, icb=icb: t.matmul(
                                ps_c[:, icb * P:(icb + 1) * P],
                                pb[2][:, icb * P:(icb + 1) * P], idn_bf[:, :],
                                start=True, stop=True,
                                skip_group_check=True), inc="pe")
                        evA[(j, "pe_vt")] = ev_tp
                        pr.wait("scalar", ev_tp)
                        evA[(j, "s_vt")] = pr.emit("scalar", lambda s, j=j: s.copy(
                            out=vTs(j)[:, :], in_=ps_c[:, :]), inc="sc")
                ebufs = (pf[4], pf[5]) if proj == "q" else (pf[6], pf[7])
                qdst = pb[2] if proj == "q" else pb[3]
                for half in range(2):
                    psl = pst[:, half * 512:half * 512 + 512]
                    pr.wait("scalar", A(i, f"pe_{proj}2h{half}"))
                    if i == 0 and half == 0:
                        pr.wait("scalar", ev_stat)
                        pr.wait("scalar", ev_brow)
                    if proj == "q" and half == 0:
                        pr.wait("scalar", A(i - 1, "pe_vt"))
                    if i == 0 and proj == "k" and half == 0:
                        pr.wait("scalar", ev_prev_w2_reader)

                    def _elu_s(s, psl=psl, ebufs=ebufs, half=half):
                        s.activation(out=ebufs[0][:, half * 512:half * 512 + 512],
                                     in_=psl, func=ACT.Relu, scale=sv["n1"][:, 0:1])
                        return s.activation(
                            out=ebufs[1][:, half * 512:half * 512 + 512],
                            in_=ebufs[0][:, half * 512:half * 512 + 512],
                            func=ACT.Exp, scale=sv["n1"][:, 0:1])
                    evA[(i, f"s_elu{proj}{half}")] = pr.emit("scalar", _elu_s,
                                                             inc="sc")
                    pr.wait("vector", A(i, f"s_elu{proj}{half}"))
                    if proj == "q" and half == 0:
                        pr.wait("vector", A(i - 1, "pe_vt"))
                        pr.wait("vector", A(i - 1, "v_vp1"))
                    if proj == "k" and half == 0:
                        pr.wait("vector", A(i - 1, "v_qks"))
                    if i == 0 and proj == "k" and half == 0:
                        pr.wait("vector", ev_prev_w2_reader)
                    _stt = lambda v, psl=psl, ebufs=ebufs, qdst=qdst, half=half: v.scalar_tensor_tensor(
                        out=qdst[:, half * 512:half * 512 + 512], in0=psl,
                        scalar=0.0,
                        in1=ebufs[1][:, half * 512:half * 512 + 512],
                        op0=ALU.max, op1=ALU.add)
                    if half == 1:
                        evA[(i, f"v_elu{proj}{half}")] = vemit(_stt)
                    else:
                        evA[(i, f"v_elu{proj}{half}")] = pr.emit("vector", _stt,
                                                                 inc="v")

            pr.wait("tensor", A(i - 1, "s_vt"))
            for half in range(2):
                psl = ps_c[:, half * 512:half * 512 + 512]
                for ic in range(NCH):
                    pr.emit("tensor", lambda t, psl=psl, ic=ic, half=half, xT=xT: t.matmul(
                        psl, xT[:, ic * P:(ic + 1) * P],
                        wvv[:, ic * D + half * 512: ic * D + half * 512 + 512],
                        start=(ic == 0), stop=False, skip_group_check=True))
                evA[(i, f"pe_v2h{half}")] = pr.emit(
                    "tensor", lambda t, psl=psl, half=half: t.matmul(
                        psl, ones128[:, :],
                        brep[:, 2 * D + half * 512: 2 * D + half * 512 + 512],
                        start=False, stop=True, skip_group_check=True), inc="pe")

            if i + 1 < NT:
                pr.wait("tensor", A(i + 1, "v_x1"))
                pr.wait("tensor", A(i, "v_eluk1"))
                emit_x1t(i + 1, ps_b)

            pr.wait("tensor", A(i, "v_eluq1"))
            pr.wait("tensor", A(i, "v_eluk1"))
            ev_smm = None
            for c in range(NCH):
                st = (i == 0 and c in (0, 4))
                ev_smm = pr.emit("tensor", lambda t, c=c, st=st, i=i: t.matmul(
                    ps_s[:, c * P:(c + 1) * P], pb[3][:, c * P:(c + 1) * P],
                    pb[2][:, c * P:(c + 1) * P], start=st,
                    stop=(i == NT - 1 and c in (3, 7)),
                    skip_group_check=True), inc="pe")
            evA[(i, "pe_smm")] = ev_smm

            if i == 0:
                pr.wait("vector", ev_bcast_mm[3])
                pr.wait("vector", ev_prev_w2_reader)
            pr.wait("vector", A(i, "pe_smm"))

            def _qks(v):
                v.tensor_tensor(out=pb[3][:, :], in0=pb[2][:, :], in1=pb[3][:, :],
                                op=ALU.mult)
                v.tensor_reduce(out=RB[:, 0:16], in_=pb[3][:, :].rearrange(
                    "p (h e) -> p h e", h=16), axis=AX.X, op=ALU.add)
                return v.tensor_copy(out=vflush[:, :], in_=ones128[:, :])
            evA[(i, "v_qks")] = pr.emit("vector", _qks, inc="v")
            pr.wait("scalar", A(i, "v_qks"))

            def _srecip(s):
                s.activation(out=RB[:, 32:48], in_=RB[:, 0:16], func=ACT.Ln,
                             bias=sv["eqk"][:, 0:1])
                s.activation(out=RB[:, 16:32], in_=RB[:, 32:48],
                             func=ACT.Exp, scale=sv["n1"][:, 0:1])
                return s.copy(out=vflush[:, :], in_=ones128[:, :])
            evA[(i, "v_recip")] = pr.emit("scalar", _srecip, inc="sc")

            for half in range(2):
                pr.wait("vector", A(i, f"pe_v2h{half}"))
                if half == 0:
                    pr.wait("vector", A(i, "v_recip"))

                def _vph(v, half=half):
                    r = None
                    for hh in range(8):
                        h = half * 8 + hh
                        r = v.tensor_scalar(
                            out=pb[2][:, h * 64:(h + 1) * 64],
                            in0=ps_c[:, h * 64:(h + 1) * 64],
                            scalar1=RB[:, 16 + h:17 + h], scalar2=None,
                            op0=ALU.mult)
                    return r
                if half == 1:
                    evA[(i, "v_vp1")] = vemit(_vph)
                else:
                    evA[(i, "v_vp0")] = pr.emit("vector", _vph, inc="v")

            if i + 2 < NT:
                pr.wait("sync", A(i + 1, "v_x1"))
                pr.wait("sync", A(i + 1, "s_sq"))
                evA[(i + 2, "dx")] = pr.emit("sync", lambda s, i=i: s.dma_start(
                    out=pf[i % 2][:, :], in_=x_t[i + 2, :, :]), inc="dx", inc_by=16)

        pr.wait("tensor", evA[(NT - 1, "v_vp1")])
        ev_tp = None
        for icb in range(NCH):
            ev_tp = pr.emit("tensor", lambda t, icb=icb: t.matmul(
                ps_c[:, icb * P:(icb + 1) * P],
                pb[2][:, icb * P:(icb + 1) * P], idn_bf[:, :],
                start=True, stop=True, skip_group_check=True), inc="pe")
        evA[(NT - 1, "pe_vt")] = ev_tp
        pr.wait("scalar", ev_tp)
        evA[(NT - 1, "s_vt")] = pr.emit("scalar", lambda s: s.copy(
            out=vTs(NT - 1)[:, :], in_=ps_c[:, :]), inc="sc")

        pr.wait("vector", evA[(NT - 1, "pe_smm")])

        def _sevac(v):
            r = None
            for h in range(16):
                c = h // 2
                off = (h % 2) * 64
                r = v.tensor_copy(
                    out=S_sb[off:off + 64, c * P + off:c * P + off + 64],
                    in_=ps_s[off:off + 64, c * P + off:c * P + off + 64])
            return r
        ev_sevac = vemit(_sevac)

        evB = {}

        def B(i, key):
            return evB.get((i, key))

        def emit_O(j):
            obank = ps_s if j % 2 == 0 else ps_b
            osb_j = pb[0] if j % 2 == 0 else pb[1]
            pr.wait("tensor", ev_sevac)
            pr.wait("tensor", A(j, "s_vt"))
            pr.wait("tensor", B(j - 2, "s_oevac"))
            if j == 1:
                pr.wait("tensor", evA[(NT - 1, "v_eluk1")])
            ev_o = None
            for c in range(NCH):
                ev_o = pr.emit("tensor", lambda t, c=c, j=j, obank=obank: t.matmul(
                    obank[:, c * P:(c + 1) * P], S_sb[:, c * P:(c + 1) * P],
                    vTs(j)[:, c * P:(c + 1) * P],
                    start=True, stop=True, skip_group_check=True), inc="pe")
            evB[(j, "pe_o")] = ev_o
            pr.wait("scalar", ev_o)
            evB[(j, "s_oevac")] = pr.emit(
                "scalar", lambda s, obank=obank, osb_j=osb_j: s.copy(
                    out=osb_j[:, :], in_=obank[:, :]), inc="sc")

        emit_O(0)
        for i in range(NT + 2):
            if i < NT:
                xb = pf[i % 2]
                abank = ps_a if i % 2 == 0 else ps_c
                yb = pf[2 + i % 2]
                sY = svY[i % 2]
                pr.wait("sync", B(i - 2, "v_xm"))
                if i <= 1:
                    pr.wait("sync", evA[(NT - 2 + i, "v_x1")])
                    pr.wait("sync", evA[(NT - 2 + i, "s_sq")])
                evB[(i, "dx")] = pr.emit("sync", lambda s, i=i, xb=xb: s.dma_start(
                    out=xb[:, :], in_=x_t[i, :, :]), inc="dx", inc_by=16)

                if i + 1 < NT:
                    emit_O(i + 1)

                pr.wait("tensor", B(i, "s_oevac"))
                pr.wait("tensor", B(i - 2, "v_y"))
                if i <= 1:
                    pr.wait("tensor", evA[(NT - 2 + i, "v_eluq1")])
                    pr.wait("tensor", evA[(NT - 2 + i, "s_eluq1")])
                if i == 1:
                    pr.wait("tensor", evA[(NT - 1, "s_vt")])
                osb = pb[0] if i % 2 == 0 else pb[1]
                for half in range(2):
                    psl = abank[:, half * 512:half * 512 + 512]
                    for ec in range(NCH):
                        pr.emit("tensor", lambda t, psl=psl, osb=osb, ec=ec, half=half: t.matmul(
                            psl, osb[:, ec * P:(ec + 1) * P],
                            wo_s[:, ec * D + half * 512: ec * D + half * 512 + 512],
                            start=(ec == 0), stop=False, skip_group_check=True))
                    evB[(i, f"pe_a{half}")] = pr.emit(
                        "tensor", lambda t, psl=psl, half=half: t.matmul(
                            psl, ones128[:, :],
                            brep[:, 3 * D + half * 512: 3 * D + half * 512 + 512],
                            start=False, stop=True, skip_group_check=True),
                        inc="pe")

                pr.wait("vector", B(i, "dx"))
                pr.wait("vector", B(i, "pe_a1"))
                pr.wait("vector", B(i - 2, "v_t2"))
                evB[(i, "v_y")] = vemit(
                    lambda v, xb=xb, abank=abank, yb=yb, i=i: v.scalar_tensor_tensor(
                        out=yb[:, :], in0=xb[:, :], scalar=rstd_all[:, i:i + 1],
                        in1=abank[:, :], op0=ALU.mult, op1=ALU.add))
                pr.wait("scalar", B(i, "v_y"))
                pr.wait("scalar", B(i - 1, "v_yvar"))
                if i == 0:
                    pr.wait("scalar", evA[(NT - 1, "v_qks")])
                pr.emit("scalar", lambda s, i=i, yb=yb, sY=sY: s.activation(
                    out=vTs(i)[:, :], in_=yb[:, :], func=ACT.Square,
                    accum_out=sY["sumsq"]))
                evB[(i, "s_ysq")] = pr.emit("scalar", lambda s, i=i, yb=yb, sY=sY: s.activation(
                    out=vTs(i)[:, :], in_=yb[:, :], func=ACT.Identity,
                    accum_out=sY["xsum"]), inc="sc")

            j = i - 1
            if 0 <= j < NT:
                xbj = pf[j % 2]
                ybj = pf[2 + j % 2]
                xmj = pf[6 + j % 2]
                sY = svY[j % 2]
                sM = svM[j % 2]
                pr.wait("vector", B(j, "s_ysq"))

                def _ysm(v, sY=sY):
                    v.tensor_scalar(out=sY["m"], in0=sY["xsum"],
                                    scalar1=1.0 / D, scalar2=None, op0=ALU.mult)
                    v.tensor_scalar(out=sY["var0"], in0=sY["sumsq"],
                                    scalar1=1.0 / D, scalar2=None, op0=ALU.mult)
                    v.tensor_scalar(out=sY["m2"], in0=sY["m"],
                                    scalar1=sY["m"], scalar2=None,
                                    op0=ALU.mult)
                    v.tensor_copy(out=sv["dmy"][:, :], in_=sv["e5"][:, :])
                    return v.tensor_tensor(out=sY["var"], in0=sY["var0"],
                                           in1=sY["m2"], op=ALU.subtract)
                evB[(j, "v_yvar")] = pr.emit("vector", _ysm, inc="v")
                pr.wait("scalar", B(j, "v_yvar"))

                def _yrstd(s, sY=sY):
                    s.activation(out=sY["lnv"], in_=sY["var"],
                                 func=ACT.Ln, bias=sv["e3"][:, 0:1])
                    return s.activation(out=sY["negmr"], in_=sY["lnv"],
                                        func=ACT.Exp, scale=sv["nh"][:, 0:1])
                evB[(j, "s_yrstd")] = pr.emit("scalar", _yrstd, inc="sc")

                pr.wait("vector", B(j, "s_yrstd"))
                evB[(j, "v_t1")] = pr.emit("vector", lambda v, ybj=ybj, sY=sY: v.scalar_tensor_tensor(
                    out=pf[4][:, :], in0=ybj[:, :], scalar=sY["m"],
                    in1=gam_b[:, :], op0=ALU.subtract, op1=ALU.mult), inc="v")
                evB[(j, "v_t2")] = pr.emit("vector", lambda v, ybj=ybj, sY=sY: v.scalar_tensor_tensor(
                    out=ybj[:, :], in0=pf[4][:, :], scalar=sY["negmr"],
                    in1=bet_b[:, :], op0=ALU.mult, op1=ALU.add), inc="v")
                pr.wait("vector", B(j - 2, "ds_xm"))
                pr.wait("vector", B(j - 2, "s_msq"))
                if j <= 1:
                    pr.wait("vector", evA[(NT - 2 + j, "v_eluk1")])
                    pr.wait("vector", evA[(NT - 2 + j, "s_eluk1")])
                evB[(j, "v_xm")] = vemit(lambda v, xmj=xmj, ybj=ybj, xbj=xbj: v.tensor_tensor(
                    out=xmj[:, :], in0=ybj[:, :], in1=xbj[:, :], op=ALU.add))
                pr.wait("scalar", B(j, "v_xm"))
                evB[(j, "ds_xm")] = pr.emit("scalar", lambda s, j=j, xmj=xmj: s.dma_start(
                    out=xm_t[j, :, :], in_=xmj[:, :]), inc="ds", inc_by=16)
                pr.wait("scalar", B(j - 1, "v_mvar"))
                pr.emit("scalar", lambda s, j=j, xmj=xmj, sM=sM: s.activation(
                    out=vTs(j)[:, :], in_=xmj[:, :], func=ACT.Square,
                    accum_out=sM["sumsq"]))
                evB[(j, "s_msq")] = pr.emit("scalar", lambda s, j=j, xmj=xmj, sM=sM: s.activation(
                    out=vTs(j)[:, :], in_=xmj[:, :], func=ACT.Identity,
                    accum_out=sM["xsum"]), inc="sc")

            k = i - 2
            if 0 <= k < NT:
                xmk = pf[6 + k % 2]
                sM = svM[k % 2]
                pr.wait("vector", B(k, "s_msq"))

                def _msm(v, sM=sM):
                    v.tensor_scalar(out=sM["m"], in0=sM["xsum"],
                                    scalar1=1.0 / D, scalar2=None, op0=ALU.mult)
                    v.tensor_scalar(out=sM["var0"], in0=sM["sumsq"],
                                    scalar1=1.0 / D, scalar2=None, op0=ALU.mult)
                    v.tensor_scalar(out=sM["m2"], in0=sM["m"],
                                    scalar1=sM["m"], scalar2=None,
                                    op0=ALU.mult)
                    v.tensor_copy(out=sv["dmy"][:, :], in_=sv["e5"][:, :])
                    return v.tensor_tensor(out=sM["var"], in0=sM["var0"],
                                           in1=sM["m2"], op=ALU.subtract)
                evB[(k, "v_mvar")] = pr.emit("vector", _msm, inc="v")
                pr.wait("scalar", B(k, "v_mvar"))

                def _mrstd(s, sM=sM):
                    s.activation(out=sM["lnv"], in_=sM["var"],
                                 func=ACT.Ln, bias=sv["e10"][:, 0:1],
                                 scale=sv["s1p"][:, 0:1])
                    return s.activation(out=sM["negmr"], in_=sM["lnv"],
                                        func=ACT.Exp, scale=sv["nh"][:, 0:1])
                evB[(k, "s_mrstd")] = pr.emit("scalar", _mrstd, inc="sc")

                pr.wait("vector", B(k, "s_mrstd"))
                pr.wait("vector", B(k, "ds_xm"))
                evB[(k, "v_t3")] = pr.emit("vector", lambda v, xmk=xmk, sM=sM: v.scalar_tensor_tensor(
                    out=xmk[:, :], in0=xmk[:, :], scalar=sM["m"],
                    in1=gamF_b[:, :], op0=ALU.subtract, op1=ALU.mult), inc="v")
                pr.wait("vector", B(k - 1, "ds_xn"))
                evB[(k, "v_xn")] = vemit(lambda v, xmk=xmk, sM=sM: v.scalar_tensor_tensor(
                    out=pf[5][:, :], in0=xmk[:, :], scalar=sM["negmr"],
                    in1=betF_b[:, :], op0=ALU.mult, op1=ALU.add))
                pr.wait("scalar", B(k, "v_xn"))
                evB[(k, "ds_xn")] = pr.emit("scalar", lambda s, k=k: s.dma_start(
                    out=xn_t[k, :, :], in_=pf[5][:, :]), inc="ds", inc_by=16)

                def _amax(v):
                    v.reduce_max(sv["amax_t"][:, :], pf[5][:, :], axis=AX.X,
                                 apply_absolute_value=True)
                    v.tensor_copy(out=sv["dmy"][:, :], in_=sv["e5"][:, :])
                    return v.tensor_tensor(out=sv["runmax"][:, :],
                                           in0=sv["runmax"][:, :],
                                           in1=sv["amax_t"][:, :], op=ALU.max)
                evB[(k, "v_amax")] = pr.emit("vector", _amax, inc="v")

        pr.wait("tensor", evB[(NT - 1, "v_amax")])
        ev_rt = pr.emit("tensor", lambda t: t.transpose(
            ps_b[0:1, 0:P], sv["runmax"][:, 0:1], idn_f[:, :]), inc="pe")
        pr.wait("vector", ev_rt)
        def _rmax(v):
            v.reduce_max(sv["gmax1"][0:1, 0:1], ps_b[0:1, 0:P], axis=AX.X)
            return v.tensor_copy(out=sv["dmy"][:, :], in_=sv["e5"][:, :])
        ev_rmax = pr.emit("vector", _rmax, inc="v")
        ev_r8 = vemit(lambda v: v.tensor_copy(
            out=red8[0:1, 0:1], in_=sv["gmax1"][0:1, 0:1]))
        pr.wait("sync", ev_r8)
        ev_d1 = pr.emit("sync", lambda s: s.dma_start(
            out=cc_in[0:1, :], in_=red8[0:1, :]), inc="dx", inc_by=16)
        pr.wait("gpsimd", ev_d1)
        ev_cc = pr.emit("gpsimd", lambda g: g.collective_compute(
            "AllReduce", ALU.max, replica_groups=[list(range(8))],
            ins=[cc_in.opt()], outs=[cc_out.opt()]), inc="cc")
        pr.wait("sync", ev_cc)
        ev_d2 = pr.emit("sync", lambda s: s.dma_start(
            out=sv["gmax1"][0:1, 0:1], in_=cc_out[0:1, 0:1]), inc="dx", inc_by=16)
        pr.wait("vector", ev_d2)
        pr.wait("vector", ev_alpha)

        def _scale_v(v):
            v.tensor_scalar(out=sv["gmaxc"][0:1, 0:1], in0=sv["gmax1"][0:1, 0:1],
                            scalar1=1e-8, scalar2=1.0 / QMAX, op0=ALU.max,
                            op1=ALU.mult)
            return v.tensor_copy(out=sv["dmy"][:, :], in_=sv["e5"][:, :])
        ev_scv = pr.emit("vector", _scale_v, inc="v")
        pr.wait("scalar", ev_scv)

        def _scale_s(s):
            s.activation(out=sv["sjk"][0:1, 0:1], in_=sv["gmaxc"][0:1, 0:1],
                         func=ACT.Ln)
            return s.activation(out=sv["invs1"][0:1, 0:1],
                                in_=sv["sjk"][0:1, 0:1], func=ACT.Exp,
                                scale=sv["n1"][0:1, 0:1])
        ev_scs = pr.emit("scalar", _scale_s, inc="sc")
        pr.wait("vector", ev_scs)

        def _scale(v):
            v.tensor_scalar(out=sv["as1"][0:1, 0:1], in0=sv["gmaxc"][0:1, 0:1],
                            scalar1=sv["alpha1"][0:1, 0:1], scalar2=None,
                            op0=ALU.mult)
            v.tensor_copy(out=pack2[0:1, 0:1], in_=sv["invs1"][0:1, 0:1])
            return v.tensor_copy(out=pack2[0:1, 1:2], in_=sv["as1"][0:1, 0:1])
        ev_sc2 = vemit(_scale)
        pr.wait("tensor", ev_sc2)
        pr.emit("tensor", lambda t: t.drain())
        ev_bc2 = pr.emit("tensor", lambda t: t.matmul(
            ps_b[:, 0:2], ones_f[0:1, :], pack2[0:1, 0:2],
            start=True, stop=True, skip_group_check=True), inc="pe")
        pr.emit("tensor", lambda t: t.drain())
        pr.wait("vector", ev_bc2)

        def _vecs(v):
            v.tensor_copy(out=sv["invs_v"][:, :], in_=ps_b[:, 0:1])
            return v.tensor_copy(out=sv["as_v"][:, :], in_=ps_b[:, 1:2])
        ev_vecs = pr.emit("vector", _vecs, inc="v")

        evC = {}

        def C(i, key):
            return evC.get((i, key))

        for i in range(NT + 1):
            if i < NT:
                xnb = pf[i % 2]
                xmb = pf[2 + i % 2]
                rq = pb[2] if i % 2 == 0 else pb[3]
                tb = ps_s if i % 2 == 0 else ps_b

                pr.wait("sync", C(i - 2, "v_rq"))
                if i <= 1:
                    pr.wait("sync", evB[(NT - 2 + i, "v_y")])
                    pr.wait("sync", evB[(NT - 2 + i, "v_xm")])
                pr.wait("sync", evB[(i, "ds_xn")])
                evC[(i, "dxn")] = pr.emit("sync", lambda s, i=i, xnb=xnb: s.dma_start(
                    out=xnb[:, :], in_=xn_t[i, :, :]), inc="dx", inc_by=16)
                pr.wait("sync", C(i - 2, "v_out"))
                if i <= 1:
                    pr.wait("sync", evB[(NT - 1, "v_xm")])
                    pr.wait("sync", evB[(NT - 1, "v_xn")])
                pr.wait("sync", evB[(i, "ds_xm")])
                evC[(i, "dxm")] = pr.emit("sync", lambda s, i=i, xmb=xmb: s.dma_start(
                    out=xmb[:, :], in_=xm_t[i, :, :]), inc="dx", inc_by=16)

                r1b = pf[6 + i % 2]
                pr.wait("scalar", C(i, "dxn"))
                pr.wait("scalar", ev_vecs)
                pr.wait("scalar", C(i - 2, "s_xqt"))
                evC[(i, "s_zp")] = pr.emit("scalar", lambda s, xnb=xnb, tb=tb: s.activation(
                    out=tb[:, :], in_=xnb[:, :], func=ACT.Identity,
                    scale=sv["invs_v"][:, 0:1], bias=sv["mgk"][:, 0:1]),
                    inc="sc")

                pr.wait("vector", C(i, "s_zp"))
                if i <= 1:
                    pr.wait("vector", evB[(NT - 2 + i, "v_xn")])
                for h in range(2):
                    pr.emit("vector", lambda v, h=h, tb=tb: v.tensor_scalar(
                        out=i32h[:, :], in0=tb[:, h * 512:h * 512 + 512],
                        scalar1=1.0, scalar2=None, op0=ALU.mult))
                    pr.emit("vector", lambda v: v.tensor_copy(
                        out=vflush[:, :], in_=ones128[:, :]))
                    pr.emit("vector", lambda v, h=h, r1b=r1b: v.tensor_copy(
                        out=r1b[:, h * 512:h * 512 + 512], in_=i32h[:, :]))

                pr.emit("vector", lambda v, tb=tb, r1b=r1b, xnb=xnb: v.tensor_tensor(
                    out=xnb[:, :], in0=tb[:, :], in1=r1b[:, :],
                    op=ALU.subtract))
                evC[(i, "v_t1")] = pr.emit(
                    "vector", lambda v, xnb=xnb, r1b=r1b: v.scalar_tensor_tensor(
                        out=xnb[:, :], in0=xnb[:, :], scalar=0.5,
                        in1=r1b[:, :], op0=ALU.is_gt, op1=ALU.add), inc="v")
                pr.wait("vector", C(i - 2, "pe_xqt"))
                if i == 0:
                    pr.wait("vector", evA[(NT - 1, "pe_vt")])
                if i == 1:
                    pr.wait("vector", evB[(NT - 1, "s_msq")])
                evC[(i, "v_rq")] = vemit(lambda v, xnb=xnb, rq=rq: v.tensor_scalar(
                    out=rq[:, :], in0=xnb[:, :], scalar1=256.0, scalar2=None,
                    op0=ALU.subtract))

            for half in range(3):
                if half < 2 and 0 <= i - 1 < NT:
                    fb = ps_a if (i - 1) % 2 == 0 else ps_c
                    xq = pb[0] if (i - 1) % 2 == 0 else pb[1]
                    if half == 0:
                        pr.wait("tensor", C(i - 1, "s_xqt"))
                        pr.wait("tensor", C(i - 3, "v_out"))
                    psl = fb[:, half * 512:half * 512 + 512]
                    ev_f = None
                    for ic in range(NCH):
                        ev_f = pr.emit("tensor", lambda t, psl=psl, xq=xq, ic=ic, half=half: t.matmul(
                            psl, xq[:, ic * P:(ic + 1) * P],
                            sT[:, ic * D + half * 512: ic * D + half * 512 + 512],
                            start=(ic == 0), stop=(ic == NCH - 1),
                            skip_group_check=True),
                            inc="pe" if ic == NCH - 1 else None)
                    evC[(i - 1, f"pe_f{half}")] = ev_f
                if half == 2 and i < NT:
                    pr.wait("tensor", C(i, "v_rq"))
                    pr.wait("tensor", C(i - 2, "s_xqt"))
                    if i == 1:
                        pr.wait("tensor", ev_vecs)
                    ev_tp = None
                    for icb in range(NCH):
                        ev_tp = pr.emit("tensor", lambda t, icb=icb, tb=tb, rq=rq: t.matmul(
                            tb[:, icb * P:(icb + 1) * P],
                            rq[:, icb * P:(icb + 1) * P], idn_bf[:, :],
                            start=True, stop=True, skip_group_check=True),
                            inc="pe")
                    evC[(i, "pe_xqt")] = ev_tp
                    pr.wait("scalar", ev_tp)
                    pr.wait("scalar", C(i - 2, "pe_f1"))
                    xq_i = pb[0] if i % 2 == 0 else pb[1]
                    evC[(i, "s_xqt")] = pr.emit("scalar", lambda s, tb=tb, xq_i=xq_i: s.copy(
                        out=xq_i[:, :], in_=tb[:, :]), inc="sc")

            j = i - 1
            if 0 <= j < NT:
                fb = ps_a if j % 2 == 0 else ps_c
                xmb_j = pf[2 + j % 2]
                ob = pf[4 + j % 2]
                pr.wait("vector", C(j, "pe_f1"))
                pr.wait("vector", C(j, "dxm"))
                pr.wait("vector", C(j - 2, "ds_out"))
                if j <= 1:
                    pr.wait("vector", evB[(NT - 2 + j, "ds_xn")])
                evC[(j, "v_out")] = vemit(
                    lambda v, fb=fb, xmb_j=xmb_j, ob=ob: v.scalar_tensor_tensor(
                        out=ob[:, :], in0=fb[:, :], scalar=sv["as_v"][:, 0:1],
                        in1=xmb_j[:, :], op0=ALU.mult, op1=ALU.add))
                pr.wait("scalar", C(j, "v_out"))
                evC[(j, "ds_out")] = pr.emit("scalar", lambda s, j=j, ob=ob: s.dma_start(
                    out=out_t[j, :, :], in_=ob[:, :]), inc="ds", inc_by=16)

        pr.wait("scalar", ("ds", pr.cnt["ds"]))
        pr.wait("sync", ("dx", pr.cnt["dx"]))
        pr.wait("gpsimd", ("dg", pr.cnt["dg"]))

        @block.sync
        def _(eng):
            pr.run("sync", eng, sems)

        @block.tensor
        def _(eng):
            pr.run("tensor", eng, sems)

        @block.vector
        def _(eng):
            pr.run("vector", eng, sems)

        @block.scalar
        def _(eng):
            pr.run("scalar", eng, sems)

        @block.gpsimd
        def _(eng):
            pr.run("gpsimd", eng, sems)

    return nc


_NC_CACHE = {}


def _get_nc(nt=32):
    if nt not in _NC_CACHE:
        _NC_CACHE[nt] = build_nc(nt)
    return _NC_CACHE[nt]


def kernel(L0_attn_x, L0_attn_Wq, L0_attn_bq, L0_attn_Wk, L0_attn_bk,
           L0_attn_Wv, L0_attn_bv, L0_attn_Wq2, L0_attn_bq2, L0_attn_Wk2,
           L0_attn_bk2, L0_attn_Wv2, L0_attn_bv2, L0_attn_Wo, L0_attn_bo,
           L0_attn_gamma, L0_attn_beta, L0_ffn_weight, L0_ffn_gamma,
           L0_ffn_beta):
    nc = _get_nc(32)
    x = np.ascontiguousarray(np.asarray(L0_attn_x, dtype=np.float32))
    w = {
        "wq": L0_attn_Wq, "wk": L0_attn_Wk, "wv": L0_attn_Wv,
        "wq2": L0_attn_Wq2, "wk2": L0_attn_Wk2, "wv2": L0_attn_Wv2,
        "wo": L0_attn_Wo, "fw": L0_ffn_weight,
        "bq": L0_attn_bq, "bk": L0_attn_bk, "bv": L0_attn_bv,
        "bq2": L0_attn_bq2, "bk2": L0_attn_bk2, "bv2": L0_attn_bv2,
        "bo": L0_attn_bo, "gam": L0_attn_gamma, "bet": L0_attn_beta,
        "gamF": L0_ffn_gamma, "betF": L0_ffn_beta,
    }
    w = {k: np.ascontiguousarray(np.asarray(v, dtype=np.float32))
         for k, v in w.items()}
    in_maps = []
    for c in range(8):
        m = {"x": np.ascontiguousarray(x[c])}
        m.update(w)
        in_maps.append(m)
    res = run_bass_kernel_spmd(nc, in_maps, core_ids=list(range(8)))
    out = np.stack([np.asarray(res.results[c]["out"]) for c in range(8)], axis=0)
    return out.astype(np.float32)


# version 24
# speedup vs baseline: 1.1760x; 1.0882x over previous
import numpy as np
from contextlib import ExitStack

import concourse.bass as bass
import concourse.mybir as mybir
from concourse.bass_utils import run_bass_kernel_spmd

F32 = mybir.dt.float32
BF16 = mybir.dt.bfloat16
I32 = mybir.dt.int32
ALU = mybir.AluOpType
ACT = mybir.ActivationFunctionType
AX = mybir.AxisListType

P = 128
D = 1024
NCH = D // P
MAGIC = 12582912.0
EPS_LN1 = 1e-5
EPS_ATT = 1e-3
EPS_QK = 1e-6
EPS_F = 1e-5
QMAX = 127.0
THRESH = 0.7

ENGS = ("tensor", "vector", "scalar", "gpsimd", "sync")
_LAST_PROG = None


class Prog:

    def __init__(self):
        self.ops = {e: [] for e in ENGS}
        self.cnt = {}

    def emit(self, eng, fn, inc=None, inc_by=1):
        if inc is not None:
            self.cnt[inc] = self.cnt.get(inc, 0) + inc_by
            ev = (inc, self.cnt[inc])
        else:
            ev = None
        self.ops[eng].append(("OP", fn, inc, inc_by))
        return ev

    def wait(self, eng, ev):
        if ev is None:
            return
        sem, val = ev
        self.ops[eng].append(("WAIT", sem, val, None))

    def run(self, eng_name, eng, sems):
        for entry in self.ops[eng_name]:
            if entry[0] == "WAIT":
                eng.wait_ge(sems[entry[1]], entry[2])
            else:
                _, fn, inc, inc_by = entry
                inst = fn(eng)
                if inc is not None:
                    inst.then_inc(sems[inc], inc_by)


def build_nc(NT=32):
    L = NT * P
    nc = bass.Bass("TRN2", target_bir_lowering=False,
                   detect_race_conditions=False)

    x_d = nc.dram_tensor("x", [L, D], F32, kind="ExternalInput").ap()
    wdr = {}
    for nm in ("wq", "wk", "wv", "wq2", "wk2", "wv2", "wo", "fw"):
        wdr[nm] = nc.dram_tensor(nm, [D, D], F32, kind="ExternalInput").ap()
    vdr = {}
    for nm in ("bq", "bk", "bv", "bq2", "bk2", "bv2", "bo", "gam", "bet",
               "gamF", "betF"):
        vdr[nm] = nc.dram_tensor(nm, [D], F32, kind="ExternalInput").ap()
    out_d = nc.dram_tensor("out", [L, D], F32, kind="ExternalOutput").ap()
    sp_xn = nc.dram_tensor("sp_xn", [L, D], F32).ap()
    sp_xm = nc.dram_tensor("sp_xm", [L, D], F32).ap()
    cc_in = nc.dram_tensor("cc_in", [1, 8], F32).ap()
    cc_out = nc.dram_tensor("cc_out", [1, 8], F32).ap()

    x_t = x_d.rearrange("(n p) d -> n p d", p=P)
    out_t = out_d.rearrange("(n p) d -> n p d", p=P)
    xn_t = sp_xn.rearrange("(n p) d -> n p d", p=P)
    xm_t = sp_xm.rearrange("(n p) d -> n p d", p=P)

    pr = Prog()
    global _LAST_PROG
    _LAST_PROG = pr

    with ExitStack() as ctx:
        E = ctx.enter_context

        wqq = E(nc.sbuf_tensor("wqq", [P, NCH * D], BF16))
        wkk = E(nc.sbuf_tensor("wkk", [P, NCH * D], BF16))
        wvv = E(nc.sbuf_tensor("wvv", [P, NCH * D], BF16))
        wo_s = E(nc.sbuf_tensor("wo_s", [P, NCH * D], BF16))
        sT = E(nc.sbuf_tensor("sT", [P, NCH * D], BF16))
        vT = E(nc.sbuf_tensor("vT", [P, max(NT - 1, 18) * D], BF16))
        gam_b = E(nc.sbuf_tensor("gam_b", [P, D], BF16))
        bet_b = E(nc.sbuf_tensor("bet_b", [P, D], BF16))
        gamF_b = E(nc.sbuf_tensor("gamF_b", [P, D], BF16))
        betF_b = E(nc.sbuf_tensor("betF_b", [P, D], BF16))
        pf = [E(nc.sbuf_tensor(f"pf{j}", [P, D], F32)) for j in range(8)]
        pb = [E(nc.sbuf_tensor(f"pb{j}", [P, D], BF16)) for j in range(5)]
        i32h = E(nc.sbuf_tensor("i32h", [P, 512], I32))
        S_sb = E(nc.sbuf_tensor("S_sb", [P, D], BF16))
        brep = E(nc.sbuf_tensor("brep", [P, 4 * D], BF16))
        ones128 = E(nc.sbuf_tensor("ones128", [P, P], BF16))
        ones_f = E(nc.sbuf_tensor("ones_f", [1, P], F32))
        idn_bf = E(nc.sbuf_tensor("idn_bf", [P, P], BF16))
        idn_f = E(nc.sbuf_tensor("idn_f", [P, P], F32))
        bq_bf = E(nc.sbuf_tensor("bq_bf", [P, 3 * NCH], BF16))
        bqb = pb[3]
        vflush = E(nc.sbuf_tensor("vflush", [P, P], F32))
        svYt = E(nc.sbuf_tensor("svYt", [P, 16], F32))
        svMt = E(nc.sbuf_tensor("svMt", [P, 16], F32))
        svY = [{}, {}]
        svM = [{}, {}]
        _svnames = ("xsum", "sumsq", "m", "m2", "var0", "var", "lnv", "negmr")
        for par in range(2):
            for ix, nm_ in enumerate(_svnames):
                svY[par][nm_] = svYt[:, par * 8 + ix: par * 8 + ix + 1]
                svM[par][nm_] = svMt[:, par * 8 + ix: par * 8 + ix + 1]
        rstd_all = E(nc.sbuf_tensor("rstd_all", [P, NT], F32))
        nm_all = E(nc.sbuf_tensor("nm_all", [P, NT], F32))
        sv = {}
        for nm_ in ("xsum", "m", "m2", "var", "var0", "dmy", "lnv", "negmr", "sumsq",
                    "wabs", "asum", "cnt", "delta_v", "ndelta_v", "tpart",
                    "alpha1", "gmax1", "gmaxc", "invs1", "as1",
                    "runmax", "amax_t", "invs_v", "as_v", "e5", "e3", "e10",
                    "s1p", "nh", "n1", "sjk", "eqk", "mgk"):
            sv[nm_] = E(nc.sbuf_tensor("sv_" + nm_, [P, 1], F32))
        red8 = E(nc.sbuf_tensor("sv_red8", [1, 8], F32))
        pack2 = E(nc.sbuf_tensor("sv_pack2", [1, 2], F32))

        ps_s = E(nc.psum_tensor("ps_s", [P, D], F32))
        ps_a = E(nc.psum_tensor("ps_a", [P, D], F32))
        ps_b = E(nc.psum_tensor("ps_b", [P, D], F32))
        ps_c = E(nc.psum_tensor("ps_c", [P, D], F32))
        psums = [ps_s, ps_a, ps_b, ps_c]

        sem_names = ["dx", "ds", "dg", "pe", "v", "sc", "g", "cc"]
        sems = {n: E(nc.semaphore("sem_" + n)) for n in sem_names}
        block = E(nc.Block())

        def vTs(j):
            return pb[3][:, :] if j == NT - 1 else vT[:, j * D:(j + 1) * D]

        WT_scr = vT[:, 0:8192]
        W2_scr = vT[:, 8192:16384]
        wcbuf = vT[:, 16384:17408]

        def bfv(ps, bank):
            return ps[:, bank * 512:(bank + 1) * 512].bitcast(BF16)

        def vemit(fn, inc="v"):
            pr.emit("vector", fn)
            return pr.emit("vector", lambda v: v.tensor_copy(
                out=vflush[:, :], in_=ones128[:, :]), inc=inc)

        ev_iota = pr.emit("gpsimd", lambda g: g.iota(
            vflush[:, :], pattern=[[-1, P]], base=0, channel_multiplier=1,
            allow_small_or_imprecise_dtypes=True), inc="g")


        pr.wait("vector", ev_iota)

        def _consts(v):
            v.memset(ones128[:, :], 1.0)
            v.memset(ones_f[0:1, :], 1.0)
            v.memset(sv["e5"][:, :], EPS_LN1)
            v.memset(sv["e3"][:, :], EPS_ATT)
            v.memset(sv["e10"][:, :], EPS_LN1 * EPS_F)
            v.memset(sv["s1p"][:, :], 1.0 + EPS_F)
            v.memset(sv["nh"][:, :], -0.5)
            v.memset(sv["n1"][:, :], -1.0)
            v.memset(sv["eqk"][:, :], EPS_QK)
            v.memset(sv["mgk"][:, :], 256.0)
            v.memset(sv["wabs"][:, :], 0.0)
            v.memset(sv["asum"][:, :], 0.0)
            v.memset(sv["cnt"][:, :], 0.0)
            v.memset(S_sb[:, :], 0.0)
            v.memset(sv["runmax"][:, :], 0.0)
            v.memset(red8[0:1, :], 0.0)
            v.tensor_scalar(out=idn_f[:, :], in0=vflush[:, :], scalar1=0.0,
                            scalar2=None, op0=ALU.is_equal)
            v.tensor_scalar(out=idn_bf[:, :], in0=vflush[:, :], scalar1=0.0,
                            scalar2=None, op0=ALU.is_equal)
            return v.tensor_copy(out=sv["dmy"][:, :], in_=sv["e5"][:, :])
        ev_consts = vemit(_consts)
        ev_idn = ev_consts

        pr.wait("gpsimd", ev_consts)
        ev_ccz = pr.emit("gpsimd", lambda g: g.dma_start(
            out=cc_in[0:1, :], in_=red8[0:1, :]), inc="dg", inc_by=16)
        pr.wait("gpsimd", ev_ccz)
        ev_ccw = pr.emit("gpsimd", lambda g: g.collective_compute(
            "AllReduce", ALU.max, replica_groups=[list(range(8))],
            ins=[cc_in.opt()], outs=[cc_out.opt()]), inc="cc")

        ev_rows = None
        for j, nm in enumerate(("gam", "bet", "gamF", "betF")):
            ev_rows = pr.emit("sync", lambda s, nm=nm, j=j: s.dma_start(
                out=pf[j][0:1, :], in_=vdr[nm][None, :]), inc="dx", inc_by=16)

        ev_bo_row = pr.emit("sync", lambda s: s.dma_start(
            out=pf[6][0:1, :], in_=vdr["bo"][None, :]), inc="dx", inc_by=16)
        ev_bsm = None
        for j, nm in enumerate(("bq", "bk", "bv")):
            ev_bsm = pr.emit("gpsimd", lambda g, nm=nm, j=j: g.dma_start(
                out=bq_bf[:, j * NCH:(j + 1) * NCH],
                in_=vdr[nm].rearrange("(c p) -> p c", p=P),
                allow_slow_non_contiguous=True), inc="dg", inc_by=16)
        ev_wo = pr.emit("gpsimd", lambda g: g.dma_start(
            out=wo_s[:, :].rearrange("p (c j) -> p c j", c=NCH),
            in_=wdr["wo"].rearrange("(c p) j -> p c j", p=P)), inc="dg", inc_by=16)

        pr.wait("tensor", ev_consts)
        pr.wait("tensor", ev_idn)
        pr.wait("tensor", ev_rows)
        bdsts = [gam_b, bet_b, gamF_b, betF_b]
        ev_bcast_mm = {}
        ev_bc = None
        for j in range(4):
            pr.wait("tensor", ev_bc)
            pr.emit("tensor", lambda t: t.drain())
            ev_mm = None
            for half in range(2):
                ev_mm = pr.emit("tensor", lambda t, j=j, half=half: t.matmul(
                    ps_s[:, half * 512:(half + 1) * 512], ones_f[0:1, :],
                    pf[j][0:1, half * 512:(half + 1) * 512],
                    start=True, stop=True, skip_group_check=True), inc="pe")
            pr.emit("tensor", lambda t: t.drain())
            ev_bcast_mm[j] = ev_mm
            pr.wait("vector", ev_mm)
            ev_bc = pr.emit("vector", lambda v, j=j: v.tensor_copy(
                out=bdsts[j][:, :], in_=ps_s[:, :]), inc="v")
        ev_bcast_evac = ev_bc

        pr.wait("tensor", ev_bo_row)
        pr.emit("tensor", lambda t: t.drain())
        ev_bom = None
        for half in range(2):
            ev_bom = pr.emit("tensor", lambda t, half=half: t.matmul(
                ps_a[:, half * 512:(half + 1) * 512], ones_f[0:1, :],
                pf[6][0:1, half * 512:(half + 1) * 512],
                start=True, stop=True, skip_group_check=True), inc="pe")
        pr.emit("tensor", lambda t: t.drain())
        pr.wait("vector", ev_bom)
        ev_borep = vemit(lambda v: v.tensor_scalar(
            out=brep[:, 3 * D:4 * D], in0=ps_a[:, :], scalar1=1.0 / P,
            scalar2=None, op0=ALU.mult))

        ev_combine_done = []
        ev_prev_w2_reader = None
        ev_prev_b2m = None
        ev_brow = None
        for widx, (w1n, w2n, wdst) in enumerate(
                (("wq", "wq2", wqq), ("wk", "wk2", wkk), ("wv", "wv2", wvv))):
            pr.wait("gpsimd", ev_prev_w2_reader)
            ev_w2 = pr.emit("gpsimd", lambda g, w2n=w2n: g.dma_start(
                out=W2_scr.rearrange("p (c j) -> p c j", c=NCH),
                in_=wdr[w2n].rearrange("(c p) j -> p c j", p=P)),
                inc="dg", inc_by=16)
            prev_evac = [ev_prev_w2_reader, ev_prev_w2_reader]
            prev_tp_last = None
            for ic in range(NCH):
                pr.wait("gpsimd", prev_tp_last)
                ev_chunk = pr.emit("gpsimd", lambda g, w1n=w1n, ic=ic: g.dma_start(
                    out=wcbuf.rearrange("p (o j) -> p o j", o=1),
                    in_=wdr[w1n][ic * P:(ic + 1) * P, :].rearrange(
                        "p (o j) -> p o j", o=1)), inc="dg", inc_by=16)
                pr.wait("tensor", ev_chunk)
                for batch in range(2):
                    pr.wait("tensor", prev_evac[batch])
                    ev_tp = None
                    for jj in range(4):
                        jc = batch * 4 + jj
                        ev_tp = pr.emit("tensor", lambda t, jc=jc, jj=jj, batch=batch: t.matmul(
                            ps_c[:, batch * 512 + jj * P:batch * 512 + (jj + 1) * P],
                            wcbuf[:, jc * P:(jc + 1) * P], idn_bf[:, :],
                            start=True, stop=True, skip_group_check=True), inc="pe")
                    prev_tp_last = ev_tp
                    pr.wait("scalar", ev_tp)
                    ev_ev = pr.emit("scalar", lambda s, ic=ic, batch=batch: s.copy(
                        out=WT_scr.rearrange("p (j i) -> p j i", j=NCH)[
                            :, batch * 4:(batch + 1) * 4, ic * P:(ic + 1) * P],
                        in_=ps_c[:, batch * 512:batch * 512 + 512].rearrange(
                            "p (j i) -> p j i", j=4)), inc="sc")
                    prev_evac[batch] = ev_ev
            pr.wait("tensor", ev_w2)
            pr.wait("tensor", prev_evac[0])
            pr.wait("tensor", prev_evac[1])
            ev_evac_prev = {}
            ev_last_combine_mm = None
            for rnd in range(2):
                evs_last = {}
                for jc in range(NCH):
                    for ibl in range(4):
                        ib = rnd * 4 + ibl
                        for half in range(2):
                            slot = ibl * 2 + half
                            psl = psums[slot // 2][
                                :, (slot % 2) * 512:(slot % 2) * 512 + 512]
                            if jc == 0:
                                pr.wait("tensor", ev_evac_prev.get(slot))
                            ev_mm = pr.emit("tensor", lambda t, psl=psl, jc=jc, ib=ib, half=half, wdst=wdst: t.matmul(
                                psl,
                                WT_scr[:, jc * D + ib * P: jc * D + (ib + 1) * P],
                                W2_scr[:, jc * D + half * 512: jc * D + half * 512 + 512],
                                start=(jc == 0), stop=(jc == NCH - 1),
                                skip_group_check=True), inc="pe")
                            evs_last[slot] = ev_mm
                ev_last_combine_mm = ev_mm
                for ibl in range(4):
                    ib = rnd * 4 + ibl
                    for half in range(2):
                        slot = ibl * 2 + half
                        psl = psums[slot // 2][
                            :, (slot % 2) * 512:(slot % 2) * 512 + 512]
                        pr.wait("vector", evs_last[slot])
                        ev_e = pr.emit("vector", lambda v, psl=psl, wdst=wdst, ib=ib, half=half: v.tensor_copy(
                            out=wdst[:, ib * D + half * 512: ib * D + half * 512 + 512],
                            in_=psl), inc="v")
                        ev_evac_prev[slot] = ev_e
                        ev_combine_done.append(ev_e)
            pr.wait("vector", ev_prev_w2_reader)
            pr.wait("vector", ev_bsm)
            ev_bqb = vemit(lambda v, widx=widx: v.tensor_copy(
                out=bqb[:, :].rearrange("p (c m) -> p m c", c=NCH),
                in_=bq_bf[:, widx * NCH:(widx + 1) * NCH].unsqueeze(1).broadcast_to(
                    [P, P, NCH])))
            b2n = ("bq2", "bk2", "bv2")[widx]
            pr.wait("sync", ev_prev_b2m)
            ev_b2 = pr.emit("sync", lambda s, b2n=b2n: s.dma_start(
                out=pf[5][0:1, :], in_=vdr[b2n][None, :]), inc="dx", inc_by=16)
            pr.wait("tensor", ev_b2)
            pr.wait("tensor", ev_evac_prev.get(2))
            pr.wait("tensor", ev_evac_prev.get(3))
            if widx == 0:
                pr.wait("tensor", ev_borep)
            pr.emit("tensor", lambda t: t.drain())
            ev_b2m = None
            for half in range(2):
                ev_b2m = pr.emit("tensor", lambda t, half=half: t.matmul(
                    ps_a[:, half * 512:(half + 1) * 512], ones_f[0:1, :],
                    pf[5][0:1, half * 512:(half + 1) * 512],
                    start=True, stop=True, skip_group_check=True), inc="pe")
            pr.emit("tensor", lambda t: t.drain())
            ev_prev_b2m = ev_b2m
            pr.wait("vector", ev_b2m)
            ev_b2r = vemit(lambda v: v.tensor_scalar(
                out=pf[6][:, :], in0=ps_a[:, :], scalar1=1.0 / P,
                scalar2=None, op0=ALU.mult))
            pr.wait("tensor", ev_bqb)
            pr.wait("tensor", ev_evac_prev[0])
            pr.wait("tensor", ev_evac_prev[1])
            evb = None
            for jc in range(NCH):
                for half in range(2):
                    psl = ps_s[:, half * 512:half * 512 + 512]
                    evb = pr.emit("tensor", lambda t, psl=psl, jc=jc, half=half: t.matmul(
                        psl, bqb[:, jc * P:(jc + 1) * P],
                        W2_scr[:, jc * D + half * 512: jc * D + half * 512 + 512],
                        start=(jc == 0), stop=(jc == NCH - 1),
                        skip_group_check=True), inc="pe" if jc == NCH - 1 else None)
            ev_prev_w2_reader = evb
            pr.wait("vector", evb)
            pr.wait("vector", ev_b2r)

            def _bias_fin(v, widx=widx):
                return v.scalar_tensor_tensor(
                    out=brep[:, widx * D:(widx + 1) * D], in0=ps_s[:, :],
                    scalar=1.0 / P, in1=pf[6][:, :], op0=ALU.mult, op1=ALU.add)
            ev_brow = vemit(_bias_fin)
            ev_combine_done.append(ev_brow)
            pr.wait("tensor", ev_brow)

        ev_p1 = {}
        ev_red = None
        for och in range(NCH):
            cc = och % 2
            buf = pf[6 + cc]
            pr.wait("sync", ev_p1.get(cc))
            if och < 2:
                pr.wait("sync", ev_brow)
            ev_l = pr.emit("sync", lambda s, och=och, buf=buf: s.dma_start(
                out=buf[:, :], in_=wdr["fw"][och * P:(och + 1) * P, :]),
                inc="dx", inc_by=16)
            pr.wait("vector", ev_l)

            def _red(v, buf=buf):
                v.reduce_sum(sv["m2"][:, :], buf[:, :], axis=AX.X,
                             apply_absolute_value=True)
                v.tensor_copy(out=sv["dmy"][:, :], in_=sv["e5"][:, :])
                return v.tensor_tensor(out=sv["wabs"][:, :], in0=sv["wabs"][:, :],
                                       in1=sv["m2"][:, :], op=ALU.add)
            ev_red = pr.emit("vector", _red, inc="v")
            ev_p1[cc] = ev_red
        pr.wait("tensor", ev_red)
        pr.wait("tensor", ev_brow)
        ev_tp = pr.emit("tensor", lambda t: t.transpose(
            ps_b[0:1, 0:P], sv["wabs"][:, 0:1], idn_f[:, :]), inc="pe")
        pr.wait("vector", ev_tp)

        def _delta(v):
            v.reduce_sum(sv["tpart"][0:1, 0:1], ps_b[0:1, 0:P], axis=AX.X)
            v.tensor_copy(out=sv["dmy"][:, :], in_=sv["e5"][:, :])
            return v.tensor_scalar(out=sv["tpart"][0:1, 0:1],
                                   in0=sv["tpart"][0:1, 0:1],
                                   scalar1=THRESH / (D * D), scalar2=None,
                                   op0=ALU.mult)
        ev_dl = vemit(_delta)
        pr.wait("tensor", ev_dl)
        pr.emit("tensor", lambda t: t.drain())
        ev_dbc = pr.emit("tensor", lambda t: t.matmul(
            ps_b[:, 0:1], ones_f[0:1, :], sv["tpart"][0:1, 0:1],
            start=True, stop=True, skip_group_check=True), inc="pe")
        pr.emit("tensor", lambda t: t.drain())
        pr.wait("vector", ev_dbc)

        def _dvec(v):
            v.tensor_copy(out=sv["delta_v"][:, :], in_=ps_b[:, 0:1])
            v.tensor_copy(out=sv["dmy"][:, :], in_=sv["e5"][:, :])
            return v.tensor_scalar(out=sv["ndelta_v"][:, :], in0=sv["delta_v"][:, :],
                                   scalar1=-1.0, scalar2=None, op0=ALU.mult)
        ev_dv = pr.emit("vector", _dvec, inc="v")

        prev_sT_evac = [None, None]
        prev_s_tp = None
        ev_stat = None
        ev_sgn = None
        for och in range(NCH):
            cc = och % 2
            buf = pf[6 + cc]
            pr.wait("sync", ev_stat if och >= 2 else ev_red)
            ev_l = pr.emit("sync", lambda s, och=och, buf=buf: s.dma_start(
                out=buf[:, :], in_=wdr["fw"][och * P:(och + 1) * P, :]),
                inc="dx", inc_by=16)
            pr.wait("vector", ev_l)
            pr.wait("vector", ev_dv)
            pr.wait("vector", prev_s_tp)

            def _sgn(v, buf=buf):
                v.tensor_scalar(out=pf[5][:, :], in0=buf[:, :],
                                scalar1=sv["delta_v"][:, 0:1], scalar2=None,
                                op0=ALU.is_gt)
                v.tensor_scalar(out=pf[0][:, :], in0=buf[:, :],
                                scalar1=sv["ndelta_v"][:, 0:1], scalar2=None,
                                op0=ALU.is_lt)
                return v.tensor_tensor(out=pb[0][:, :], in0=pf[5][:, :],
                                       in1=pf[0][:, :], op=ALU.subtract)
            ev_sgn = vemit(_sgn)

            def _stats(v, buf=buf):
                v.tensor_tensor(out=pf[5][:, :], in0=buf[:, :], in1=pb[0][:, :],
                                op=ALU.mult)
                v.reduce_sum(sv["m2"][:, :], pf[5][:, :], axis=AX.X)
                v.reduce_sum(sv["var0"][:, :], pb[0][:, :], axis=AX.X,
                             apply_absolute_value=True)
                v.tensor_tensor(out=sv["asum"][:, :], in0=sv["asum"][:, :],
                                in1=sv["m2"][:, :], op=ALU.add)
                return v.tensor_tensor(out=sv["cnt"][:, :], in0=sv["cnt"][:, :],
                                       in1=sv["var0"][:, :], op=ALU.add)
            ev_stat = pr.emit("vector", _stats, inc="v")
            pr.wait("tensor", ev_sgn)
            for batch in range(2):
                pr.wait("tensor", prev_sT_evac[batch])
                ev_tp = None
                for jj in range(4):
                    icb = batch * 4 + jj
                    ev_tp = pr.emit("tensor", lambda t, icb=icb, jj=jj, batch=batch: t.matmul(
                        ps_c[:, batch * 512 + jj * P:batch * 512 + (jj + 1) * P],
                        pb[0][:, icb * P:(icb + 1) * P], idn_bf[:, :],
                        start=True, stop=True, skip_group_check=True), inc="pe")
                prev_s_tp = ev_tp
                pr.wait("scalar", ev_tp)
                ev_ev = pr.emit("scalar", lambda s, och=och, batch=batch: s.copy(
                    out=sT.rearrange("p (i o) -> p i o", i=NCH)[
                        :, batch * 4:(batch + 1) * 4, och * P:(och + 1) * P],
                    in_=ps_c[:, batch * 512:batch * 512 + 512].rearrange(
                        "p (i o) -> p i o", i=4)), inc="sc")
                prev_sT_evac[batch] = ev_ev
        ev_sT_done = list(prev_sT_evac)
        ev_sT_tp_last = prev_s_tp

        pr.wait("tensor", ev_stat)
        pr.emit("tensor", lambda t: t.transpose(
            ps_b[0:1, 0:P], sv["asum"][:, 0:1], idn_f[:, :]), inc="pe")
        ev_tp3 = pr.emit("tensor", lambda t: t.transpose(
            ps_b[0:1, 128:256], sv["cnt"][:, 0:1], idn_f[:, :]), inc="pe")
        pr.wait("vector", ev_tp3)

        def _alpha_v(v):
            v.reduce_sum(sv["m2"][0:1, 0:1], ps_b[0:1, 128:256], axis=AX.X)
            v.reduce_sum(sv["alpha1"][0:1, 0:1], ps_b[0:1, 0:P], axis=AX.X)
            v.tensor_scalar(out=sv["m2"][0:1, 0:1], in0=sv["m2"][0:1, 0:1],
                            scalar1=1.0, scalar2=None, op0=ALU.max)
            return v.tensor_copy(out=sv["dmy"][:, :], in_=sv["e5"][:, :])
        ev_alphav = pr.emit("vector", _alpha_v, inc="v")
        pr.wait("scalar", ev_alphav)

        def _alpha_s(s):
            s.activation(out=sv["sjk"][0:1, 0:1], in_=sv["m2"][0:1, 0:1],
                         func=ACT.Ln)
            return s.activation(out=sv["m2"][0:1, 0:1], in_=sv["sjk"][0:1, 0:1],
                                func=ACT.Exp, scale=sv["n1"][0:1, 0:1])
        ev_alphas = pr.emit("scalar", _alpha_s, inc="sc")
        pr.wait("vector", ev_alphas)
        ev_alpha = vemit(lambda v: v.tensor_scalar(
            out=sv["alpha1"][0:1, 0:1], in0=sv["alpha1"][0:1, 0:1],
            scalar1=sv["m2"][0:1, 0:1], scalar2=None, op0=ALU.mult))

        RB = pf[3]
        evA = {}

        def A(i, key):
            return evA.get((i, key))

        def xTbuf(j):
            return pb[1] if j % 2 == 0 else pb[4]

        def emit_stats(j, dest_ev_prefix="A"):
            xb = pf[j % 2]
            pr.wait("scalar", A(j, "dx"))
            pr.wait("scalar", A(j - 1, "v_var"))
            pr.emit("scalar", lambda s, xb=xb, j=j: s.activation(
                out=vT[:, min(j, NT - 2) * D:min(j, NT - 2) * D + D],
                in_=xb[:, :], func=ACT.Square,
                accum_out=sv["sumsq"][:, 0:1]))
            evA[(j, "s_sq")] = pr.emit("scalar", lambda s, xb=xb, j=j: s.activation(
                out=vT[:, min(j, NT - 2) * D:min(j, NT - 2) * D + D],
                in_=xb[:, :], func=ACT.Identity,
                accum_out=sv["xsum"][:, 0:1]), inc="sc")
            pr.wait("vector", A(j, "s_sq"))

            def _smalls(v):
                v.tensor_scalar(out=sv["m"][:, :], in0=sv["xsum"][:, :],
                                scalar1=1.0 / D, scalar2=None, op0=ALU.mult)
                v.tensor_scalar(out=sv["var0"][:, :], in0=sv["sumsq"][:, :],
                                scalar1=1.0 / D, scalar2=None, op0=ALU.mult)
                v.tensor_scalar(out=sv["m2"][:, :], in0=sv["m"][:, :],
                                scalar1=sv["m"][:, 0:1], scalar2=None, op0=ALU.mult)
                v.tensor_copy(out=sv["dmy"][:, :], in_=sv["e5"][:, :])
                return v.tensor_tensor(out=sv["var"][:, :], in0=sv["var0"][:, :],
                                       in1=sv["m2"][:, :], op=ALU.subtract)
            evA[(j, "v_var")] = pr.emit("vector", _smalls, inc="v")
            pr.wait("scalar", A(j, "v_var"))

            def _rstd(s, j=j):
                s.activation(out=sv["lnv"][:, :], in_=sv["var"][:, :], func=ACT.Ln,
                             bias=sv["e5"][:, 0:1])
                return s.activation(out=rstd_all[:, j:j + 1], in_=sv["lnv"][:, :],
                                    func=ACT.Exp, scale=sv["nh"][:, 0:1])
            evA[(j, "s_rstd")] = pr.emit("scalar", _rstd, inc="sc")
            pr.wait("vector", A(j, "s_rstd"))
            pr.wait("vector", A(j - 1, "pe_x1t"))
            if j == 0:
                pr.wait("vector", ev_sT_tp_last)

            def _x1(v, j=j, xb=xb):
                v.tensor_scalar(out=nm_all[:, j:j + 1], in0=sv["m"][:, :],
                                scalar1=rstd_all[:, j:j + 1], scalar2=-1.0,
                                op0=ALU.mult, op1=ALU.mult)
                v.tensor_copy(out=sv["dmy"][:, :], in_=sv["e5"][:, :])
                return v.tensor_scalar(out=pb[0][:, :], in0=xb[:, :],
                                       scalar1=rstd_all[:, j:j + 1],
                                       scalar2=nm_all[:, j:j + 1],
                                       op0=ALU.mult, op1=ALU.add)
            evA[(j, "v_x1")] = vemit(_x1)

        def emit_x1t(j, bank_ps):
            ev_tp = None
            for icb in range(NCH):
                ev_tp = pr.emit("tensor", lambda t, icb=icb, bank_ps=bank_ps: t.matmul(
                    bank_ps[:, icb * P:(icb + 1) * P],
                    pb[0][:, icb * P:(icb + 1) * P], idn_bf[:, :],
                    start=True, stop=True, skip_group_check=True), inc="pe")
            evA[(j, "pe_x1t")] = ev_tp
            pr.wait("vector", ev_tp)
            evA[(j, "s_x1t")] = vemit(lambda v, j=j, bank_ps=bank_ps: v.tensor_copy(
                out=xTbuf(j)[:, :], in_=bank_ps[:, :]))

        pr.wait("sync", ev_stat)
        pr.wait("sync", ev_brow)
        for j in range(2):
            evA[(j, "dx")] = pr.emit("sync", lambda s, j=j: s.dma_start(
                out=pf[j % 2][:, :], in_=x_t[j, :, :]), inc="dx", inc_by=16)
        emit_stats(0)
        pr.wait("tensor", A(0, "v_x1"))
        for e_ in ev_sT_done:
            pr.wait("tensor", e_)
        pr.wait("tensor", ev_alpha)
        for e_ in ev_combine_done:
            pr.wait("tensor", e_)
        pr.wait("tensor", ev_bcast_evac)
        pr.wait("tensor", ev_borep)
        pr.wait("tensor", ev_wo)
        emit_x1t(0, ps_b)

        for i in range(NT):
            xT = xTbuf(i)
            if i + 1 < NT:
                emit_stats(i + 1)

            for (proj, wmat, rowoff, pst) in (("q", wqq, 0, ps_a),
                                              ("k", wkk, 1, ps_b)):
                pr.wait("tensor", A(i, "s_x1t"))
                if proj == "q":
                    pr.wait("tensor", A(i - 1, "v_eluq1"))
                else:
                    pr.wait("tensor", A(i - 1, "v_eluk1"))
                for half in range(2):
                    psl = pst[:, half * 512:half * 512 + 512]
                    for ic in range(NCH):
                        pr.emit("tensor", lambda t, psl=psl, wmat=wmat, ic=ic, half=half, xT=xT: t.matmul(
                            psl, xT[:, ic * P:(ic + 1) * P],
                            wmat[:, ic * D + half * 512: ic * D + half * 512 + 512],
                            start=(ic == 0), stop=False, skip_group_check=True))
                    evA[(i, f"pe_{proj}2h{half}")] = pr.emit(
                        "tensor", lambda t, psl=psl, rowoff=rowoff, half=half: t.matmul(
                            psl, ones128[:, :],
                            brep[:, rowoff * D + half * 512:
                                 rowoff * D + half * 512 + 512],
                            start=False, stop=True, skip_group_check=True),
                        inc="pe")
                    if proj == "q" and half == 0 and i > 0:
                        j = i - 1
                        pr.wait("tensor", A(j, "v_vp1"))
                        ev_tp = None
                        for icb in range(NCH):
                            ev_tp = pr.emit("tensor", lambda t, icb=icb: t.matmul(
                                ps_c[:, icb * P:(icb + 1) * P],
                                pb[2][:, icb * P:(icb + 1) * P], idn_bf[:, :],
                                start=True, stop=True,
                                skip_group_check=True), inc="pe")
                        evA[(j, "pe_vt")] = ev_tp
                        pr.wait("scalar", ev_tp)
                        evA[(j, "s_vt")] = pr.emit("scalar", lambda s, j=j: s.copy(
                            out=vTs(j)[:, :], in_=ps_c[:, :]), inc="sc")
                ebufs = (pf[4], pf[5]) if proj == "q" else (pf[6], pf[7])
                qdst = pb[2] if proj == "q" else pb[3]
                for half in range(2):
                    psl = pst[:, half * 512:half * 512 + 512]
                    pr.wait("scalar", A(i, f"pe_{proj}2h{half}"))
                    if i == 0 and half == 0:
                        pr.wait("scalar", ev_stat)
                        pr.wait("scalar", ev_brow)
                    if proj == "q" and half == 0:
                        pr.wait("scalar", A(i - 1, "pe_vt"))
                    if i == 0 and proj == "k" and half == 0:
                        pr.wait("scalar", ev_prev_w2_reader)

                    def _elu_s(s, psl=psl, ebufs=ebufs, half=half):
                        s.activation(out=ebufs[0][:, half * 512:half * 512 + 512],
                                     in_=psl, func=ACT.Relu, scale=sv["n1"][:, 0:1])
                        return s.activation(
                            out=ebufs[1][:, half * 512:half * 512 + 512],
                            in_=ebufs[0][:, half * 512:half * 512 + 512],
                            func=ACT.Exp, scale=sv["n1"][:, 0:1])
                    evA[(i, f"s_elu{proj}{half}")] = pr.emit("scalar", _elu_s,
                                                             inc="sc")
                    pr.wait("vector", A(i, f"s_elu{proj}{half}"))
                    if proj == "q" and half == 0:
                        pr.wait("vector", A(i - 1, "pe_vt"))
                        pr.wait("vector", A(i - 1, "v_vp1"))
                    if proj == "k" and half == 0:
                        pr.wait("vector", A(i - 1, "v_qks"))
                    if i == 0 and proj == "k" and half == 0:
                        pr.wait("vector", ev_prev_w2_reader)
                    _stt = lambda v, psl=psl, ebufs=ebufs, qdst=qdst, half=half: v.scalar_tensor_tensor(
                        out=qdst[:, half * 512:half * 512 + 512], in0=psl,
                        scalar=0.0,
                        in1=ebufs[1][:, half * 512:half * 512 + 512],
                        op0=ALU.max, op1=ALU.add)
                    if half == 1:
                        evA[(i, f"v_elu{proj}{half}")] = vemit(_stt)
                    else:
                        evA[(i, f"v_elu{proj}{half}")] = pr.emit("vector", _stt,
                                                                 inc="v")

            pr.wait("tensor", A(i - 1, "s_vt"))
            for half in range(2):
                psl = ps_c[:, half * 512:half * 512 + 512]
                for ic in range(NCH):
                    pr.emit("tensor", lambda t, psl=psl, ic=ic, half=half, xT=xT: t.matmul(
                        psl, xT[:, ic * P:(ic + 1) * P],
                        wvv[:, ic * D + half * 512: ic * D + half * 512 + 512],
                        start=(ic == 0), stop=False, skip_group_check=True))
                evA[(i, f"pe_v2h{half}")] = pr.emit(
                    "tensor", lambda t, psl=psl, half=half: t.matmul(
                        psl, ones128[:, :],
                        brep[:, 2 * D + half * 512: 2 * D + half * 512 + 512],
                        start=False, stop=True, skip_group_check=True), inc="pe")

            if i + 1 < NT:
                pr.wait("tensor", A(i + 1, "v_x1"))
                pr.wait("tensor", A(i, "v_eluk1"))
                emit_x1t(i + 1, ps_b)

            pr.wait("tensor", A(i, "v_eluq1"))
            pr.wait("tensor", A(i, "v_eluk1"))
            ev_smm = None
            for c in range(NCH):
                st = (i == 0 and c in (0, 4))
                ev_smm = pr.emit("tensor", lambda t, c=c, st=st, i=i: t.matmul(
                    ps_s[:, c * P:(c + 1) * P], pb[3][:, c * P:(c + 1) * P],
                    pb[2][:, c * P:(c + 1) * P], start=st,
                    stop=(i == NT - 1 and c in (3, 7)),
                    skip_group_check=True), inc="pe")
            evA[(i, "pe_smm")] = ev_smm

            if i == 0:
                pr.wait("vector", ev_bcast_mm[3])
                pr.wait("vector", ev_prev_w2_reader)
            pr.wait("vector", A(i, "pe_smm"))

            def _qks(v):
                v.tensor_tensor(out=pb[3][:, :], in0=pb[2][:, :], in1=pb[3][:, :],
                                op=ALU.mult)
                v.tensor_reduce(out=RB[:, 0:16], in_=pb[3][:, :].rearrange(
                    "p (h e) -> p h e", h=16), axis=AX.X, op=ALU.add)
                return v.tensor_copy(out=vflush[:, :], in_=ones128[:, :])
            evA[(i, "v_qks")] = pr.emit("vector", _qks, inc="v")
            pr.wait("scalar", A(i, "v_qks"))

            def _srecip(s):
                s.activation(out=RB[:, 32:48], in_=RB[:, 0:16], func=ACT.Ln,
                             bias=sv["eqk"][:, 0:1])
                s.activation(out=RB[:, 16:32], in_=RB[:, 32:48],
                             func=ACT.Exp, scale=sv["n1"][:, 0:1])
                return s.copy(out=vflush[:, :], in_=ones128[:, :])
            evA[(i, "v_recip")] = pr.emit("scalar", _srecip, inc="sc")

            for half in range(2):
                pr.wait("vector", A(i, f"pe_v2h{half}"))
                if half == 0:
                    pr.wait("vector", A(i, "v_recip"))

                def _vph(v, half=half):
                    r = None
                    for hh in range(8):
                        h = half * 8 + hh
                        r = v.tensor_scalar(
                            out=pb[2][:, h * 64:(h + 1) * 64],
                            in0=ps_c[:, h * 64:(h + 1) * 64],
                            scalar1=RB[:, 16 + h:17 + h], scalar2=None,
                            op0=ALU.mult)
                    return r
                if half == 1:
                    evA[(i, "v_vp1")] = vemit(_vph)
                else:
                    evA[(i, "v_vp0")] = pr.emit("vector", _vph, inc="v")

            if i + 2 < NT:
                pr.wait("sync", A(i + 1, "v_x1"))
                pr.wait("sync", A(i + 1, "s_sq"))
                evA[(i + 2, "dx")] = pr.emit("sync", lambda s, i=i: s.dma_start(
                    out=pf[i % 2][:, :], in_=x_t[i + 2, :, :]), inc="dx", inc_by=16)

        pr.wait("tensor", evA[(NT - 1, "v_vp1")])
        ev_tp = None
        for icb in range(NCH):
            ev_tp = pr.emit("tensor", lambda t, icb=icb: t.matmul(
                ps_c[:, icb * P:(icb + 1) * P],
                pb[2][:, icb * P:(icb + 1) * P], idn_bf[:, :],
                start=True, stop=True, skip_group_check=True), inc="pe")
        evA[(NT - 1, "pe_vt")] = ev_tp
        pr.wait("scalar", ev_tp)
        evA[(NT - 1, "s_vt")] = pr.emit("scalar", lambda s: s.copy(
            out=vTs(NT - 1)[:, :], in_=ps_c[:, :]), inc="sc")

        pr.wait("vector", evA[(NT - 1, "pe_smm")])

        def _sevac(v):
            r = None
            for h in range(16):
                c = h // 2
                off = (h % 2) * 64
                r = v.tensor_copy(
                    out=S_sb[off:off + 64, c * P + off:c * P + off + 64],
                    in_=ps_s[off:off + 64, c * P + off:c * P + off + 64])
            return r
        ev_sevac = vemit(_sevac)

        evB = {}

        def B(i, key):
            return evB.get((i, key))

        def emit_O(j):
            obank = ps_s if j % 2 == 0 else ps_b
            osb_j = pb[0] if j % 2 == 0 else pb[1]
            pr.wait("tensor", ev_sevac)
            pr.wait("tensor", A(j, "s_vt"))
            pr.wait("tensor", B(j - 2, "s_oevac"))
            if j == 1:
                pr.wait("tensor", evA[(NT - 1, "v_eluk1")])
            ev_o = None
            for c in range(NCH):
                ev_o = pr.emit("tensor", lambda t, c=c, j=j, obank=obank: t.matmul(
                    obank[:, c * P:(c + 1) * P], S_sb[:, c * P:(c + 1) * P],
                    vTs(j)[:, c * P:(c + 1) * P],
                    start=True, stop=True, skip_group_check=True), inc="pe")
            evB[(j, "pe_o")] = ev_o
            pr.wait("scalar", ev_o)
            evB[(j, "s_oevac")] = pr.emit(
                "scalar", lambda s, obank=obank, osb_j=osb_j: s.copy(
                    out=osb_j[:, :], in_=obank[:, :]), inc="sc")

        emit_O(0)
        for i in range(NT + 2):
            if i < NT:
                xb = pf[i % 2]
                abank = ps_a if i % 2 == 0 else ps_c
                yb = pf[2 + i % 2]
                sY = svY[i % 2]
                pr.wait("sync", B(i - 2, "v_xm"))
                if i <= 1:
                    pr.wait("sync", evA[(NT - 2 + i, "v_x1")])
                    pr.wait("sync", evA[(NT - 2 + i, "s_sq")])
                evB[(i, "dx")] = pr.emit("sync", lambda s, i=i, xb=xb: s.dma_start(
                    out=xb[:, :], in_=x_t[i, :, :]), inc="dx", inc_by=16)

                if i + 1 < NT:
                    emit_O(i + 1)

                pr.wait("tensor", B(i, "s_oevac"))
                pr.wait("tensor", B(i - 2, "v_y"))
                if i <= 1:
                    pr.wait("tensor", evA[(NT - 2 + i, "v_eluq1")])
                    pr.wait("tensor", evA[(NT - 2 + i, "s_eluq1")])
                if i == 1:
                    pr.wait("tensor", evA[(NT - 1, "s_vt")])
                osb = pb[0] if i % 2 == 0 else pb[1]
                for half in range(2):
                    psl = abank[:, half * 512:half * 512 + 512]
                    for ec in range(NCH):
                        pr.emit("tensor", lambda t, psl=psl, osb=osb, ec=ec, half=half: t.matmul(
                            psl, osb[:, ec * P:(ec + 1) * P],
                            wo_s[:, ec * D + half * 512: ec * D + half * 512 + 512],
                            start=(ec == 0), stop=False, skip_group_check=True))
                    evB[(i, f"pe_a{half}")] = pr.emit(
                        "tensor", lambda t, psl=psl, half=half: t.matmul(
                            psl, ones128[:, :],
                            brep[:, 3 * D + half * 512: 3 * D + half * 512 + 512],
                            start=False, stop=True, skip_group_check=True),
                        inc="pe")

                pr.wait("vector", B(i, "dx"))
                pr.wait("vector", B(i, "pe_a1"))
                pr.wait("vector", B(i - 2, "v_t2"))
                evB[(i, "v_y")] = vemit(
                    lambda v, xb=xb, abank=abank, yb=yb, i=i: v.scalar_tensor_tensor(
                        out=yb[:, :], in0=xb[:, :], scalar=rstd_all[:, i:i + 1],
                        in1=abank[:, :], op0=ALU.mult, op1=ALU.add))
                pr.wait("scalar", B(i, "v_y"))
                pr.wait("scalar", B(i - 2, "v_yvar"))
                if i == 0:
                    pr.wait("scalar", evA[(NT - 1, "v_qks")])
                pr.emit("scalar", lambda s, i=i, yb=yb, sY=sY: s.activation(
                    out=vTs(i)[:, :], in_=yb[:, :], func=ACT.Square,
                    accum_out=sY["sumsq"]))
                pr.emit("scalar", lambda s, i=i, yb=yb, sY=sY: s.activation(
                    out=vTs(i)[:, :], in_=yb[:, :], func=ACT.Identity,
                    accum_out=sY["xsum"]))
                evB[(i, "s_ysq")] = pr.emit("scalar", lambda s: s.copy(
                    out=vflush[:, :], in_=ones128[:, :]), inc="sc")

            j = i - 1
            if 0 <= j < NT:
                xbj = pf[j % 2]
                ybj = pf[2 + j % 2]
                xmj = pf[6 + j % 2]
                sY = svY[j % 2]
                sM = svM[j % 2]
                pr.wait("vector", B(j, "s_yrstd"))
                evB[(j, "v_t1")] = pr.emit("vector", lambda v, ybj=ybj, sY=sY: v.scalar_tensor_tensor(
                    out=pf[4][:, :], in0=ybj[:, :], scalar=sY["m"],
                    in1=gam_b[:, :], op0=ALU.subtract, op1=ALU.mult), inc="v")
                evB[(j, "v_t2")] = pr.emit("vector", lambda v, ybj=ybj, sY=sY: v.scalar_tensor_tensor(
                    out=ybj[:, :], in0=pf[4][:, :], scalar=sY["negmr"],
                    in1=bet_b[:, :], op0=ALU.mult, op1=ALU.add), inc="v")
                pr.wait("vector", B(j - 2, "ds_xm"))
                pr.wait("vector", B(j - 2, "s_msq"))
                if j <= 1:
                    pr.wait("vector", evA[(NT - 2 + j, "v_eluk1")])
                    pr.wait("vector", evA[(NT - 2 + j, "s_eluk1")])
                evB[(j, "v_xm")] = vemit(lambda v, xmj=xmj, ybj=ybj, xbj=xbj: v.tensor_tensor(
                    out=xmj[:, :], in0=ybj[:, :], in1=xbj[:, :], op=ALU.add))
                pr.wait("scalar", B(j, "v_xm"))
                evB[(j, "ds_xm")] = pr.emit("scalar", lambda s, j=j, xmj=xmj: s.dma_start(
                    out=xm_t[j, :, :], in_=xmj[:, :]), inc="ds", inc_by=16)
                pr.wait("scalar", B(j - 2, "v_mvar"))
                pr.emit("scalar", lambda s, j=j, xmj=xmj, sM=sM: s.activation(
                    out=vTs(j)[:, :], in_=xmj[:, :], func=ACT.Square,
                    accum_out=sM["sumsq"]))
                pr.emit("scalar", lambda s, j=j, xmj=xmj, sM=sM: s.activation(
                    out=vTs(j)[:, :], in_=xmj[:, :], func=ACT.Identity,
                    accum_out=sM["xsum"]))
                evB[(j, "s_msq")] = pr.emit("scalar", lambda s: s.copy(
                    out=vflush[:, :], in_=ones128[:, :]), inc="sc")

            k = i - 2
            if 0 <= k < NT:
                xmk = pf[6 + k % 2]
                sM = svM[k % 2]
                pr.wait("vector", B(k, "s_mrstd"))
                pr.wait("vector", B(k, "ds_xm"))
                evB[(k, "v_t3")] = pr.emit("vector", lambda v, xmk=xmk, sM=sM: v.scalar_tensor_tensor(
                    out=xmk[:, :], in0=xmk[:, :], scalar=sM["m"],
                    in1=gamF_b[:, :], op0=ALU.subtract, op1=ALU.mult), inc="v")
                pr.wait("vector", B(k - 1, "ds_xn"))
                evB[(k, "v_xn")] = vemit(lambda v, xmk=xmk, sM=sM: v.scalar_tensor_tensor(
                    out=pf[5][:, :], in0=xmk[:, :], scalar=sM["negmr"],
                    in1=betF_b[:, :], op0=ALU.mult, op1=ALU.add))
                pr.wait("scalar", B(k, "v_xn"))
                evB[(k, "ds_xn")] = pr.emit("scalar", lambda s, k=k: s.dma_start(
                    out=xn_t[k, :, :], in_=pf[5][:, :]), inc="ds", inc_by=16)

                def _amax(v):
                    v.reduce_max(sv["amax_t"][:, :], pf[5][:, :], axis=AX.X,
                                 apply_absolute_value=True)
                    v.tensor_copy(out=sv["dmy"][:, :], in_=sv["e5"][:, :])
                    return v.tensor_tensor(out=sv["runmax"][:, :],
                                           in0=sv["runmax"][:, :],
                                           in1=sv["amax_t"][:, :], op=ALU.max)
                evB[(k, "v_amax")] = pr.emit("vector", _amax, inc="v")

            if i < NT:
                sY = svY[i % 2]
                pr.wait("vector", B(i, "s_ysq"))

                def _ysm(v, sY=sY):
                    v.tensor_scalar(out=sY["m"], in0=sY["xsum"],
                                    scalar1=1.0 / D, scalar2=None, op0=ALU.mult)
                    v.tensor_scalar(out=sY["var0"], in0=sY["sumsq"],
                                    scalar1=1.0 / D, scalar2=None, op0=ALU.mult)
                    v.tensor_scalar(out=sY["m2"], in0=sY["m"],
                                    scalar1=sY["m"], scalar2=None,
                                    op0=ALU.mult)
                    v.tensor_copy(out=sv["dmy"][:, :], in_=sv["e5"][:, :])
                    return v.tensor_tensor(out=sY["var"], in0=sY["var0"],
                                           in1=sY["m2"], op=ALU.subtract)
                evB[(i, "v_yvar")] = pr.emit("vector", _ysm, inc="v")
                pr.wait("scalar", B(i, "v_yvar"))

                def _yrstd(s, sY=sY):
                    s.activation(out=sY["lnv"], in_=sY["var"],
                                 func=ACT.Ln, bias=sv["e3"][:, 0:1])
                    return s.activation(out=sY["negmr"], in_=sY["lnv"],
                                        func=ACT.Exp, scale=sv["nh"][:, 0:1])
                evB[(i, "s_yrstd")] = pr.emit("scalar", _yrstd, inc="sc")
            if 0 <= i - 1 < NT:
                jj = i - 1
                sM = svM[jj % 2]
                pr.wait("vector", B(jj, "s_msq"))

                def _msm(v, sM=sM):
                    v.tensor_scalar(out=sM["m"], in0=sM["xsum"],
                                    scalar1=1.0 / D, scalar2=None, op0=ALU.mult)
                    v.tensor_scalar(out=sM["var0"], in0=sM["sumsq"],
                                    scalar1=1.0 / D, scalar2=None, op0=ALU.mult)
                    v.tensor_scalar(out=sM["m2"], in0=sM["m"],
                                    scalar1=sM["m"], scalar2=None,
                                    op0=ALU.mult)
                    v.tensor_copy(out=sv["dmy"][:, :], in_=sv["e5"][:, :])
                    return v.tensor_tensor(out=sM["var"], in0=sM["var0"],
                                           in1=sM["m2"], op=ALU.subtract)
                evB[(jj, "v_mvar")] = pr.emit("vector", _msm, inc="v")
                pr.wait("scalar", B(jj, "v_mvar"))

                def _mrstd(s, sM=sM):
                    s.activation(out=sM["lnv"], in_=sM["var"],
                                 func=ACT.Ln, bias=sv["e10"][:, 0:1],
                                 scale=sv["s1p"][:, 0:1])
                    return s.activation(out=sM["negmr"], in_=sM["lnv"],
                                        func=ACT.Exp, scale=sv["nh"][:, 0:1])
                evB[(jj, "s_mrstd")] = pr.emit("scalar", _mrstd, inc="sc")

        pr.wait("tensor", evB[(NT - 1, "v_amax")])
        ev_rt = pr.emit("tensor", lambda t: t.transpose(
            ps_b[0:1, 0:P], sv["runmax"][:, 0:1], idn_f[:, :]), inc="pe")
        pr.wait("vector", ev_rt)
        def _rmax(v):
            v.reduce_max(sv["gmax1"][0:1, 0:1], ps_b[0:1, 0:P], axis=AX.X)
            return v.tensor_copy(out=sv["dmy"][:, :], in_=sv["e5"][:, :])
        ev_rmax = pr.emit("vector", _rmax, inc="v")
        ev_r8 = vemit(lambda v: v.tensor_copy(
            out=red8[0:1, 0:1], in_=sv["gmax1"][0:1, 0:1]))
        pr.wait("sync", ev_r8)
        ev_d1 = pr.emit("sync", lambda s: s.dma_start(
            out=cc_in[0:1, :], in_=red8[0:1, :]), inc="dx", inc_by=16)
        pr.wait("gpsimd", ev_d1)
        ev_cc = pr.emit("gpsimd", lambda g: g.collective_compute(
            "AllReduce", ALU.max, replica_groups=[list(range(8))],
            ins=[cc_in.opt()], outs=[cc_out.opt()]), inc="cc")
        pr.wait("sync", ev_cc)
        ev_d2 = pr.emit("sync", lambda s: s.dma_start(
            out=sv["gmax1"][0:1, 0:1], in_=cc_out[0:1, 0:1]), inc="dx", inc_by=16)
        pr.wait("vector", ev_d2)
        pr.wait("vector", ev_alpha)

        def _scale_v(v):
            v.tensor_scalar(out=sv["gmaxc"][0:1, 0:1], in0=sv["gmax1"][0:1, 0:1],
                            scalar1=1e-8, scalar2=1.0 / QMAX, op0=ALU.max,
                            op1=ALU.mult)
            return v.tensor_copy(out=sv["dmy"][:, :], in_=sv["e5"][:, :])
        ev_scv = pr.emit("vector", _scale_v, inc="v")
        pr.wait("scalar", ev_scv)

        def _scale_s(s):
            s.activation(out=sv["sjk"][0:1, 0:1], in_=sv["gmaxc"][0:1, 0:1],
                         func=ACT.Ln)
            return s.activation(out=sv["invs1"][0:1, 0:1],
                                in_=sv["sjk"][0:1, 0:1], func=ACT.Exp,
                                scale=sv["n1"][0:1, 0:1])
        ev_scs = pr.emit("scalar", _scale_s, inc="sc")
        pr.wait("vector", ev_scs)

        def _scale(v):
            v.tensor_scalar(out=sv["as1"][0:1, 0:1], in0=sv["gmaxc"][0:1, 0:1],
                            scalar1=sv["alpha1"][0:1, 0:1], scalar2=None,
                            op0=ALU.mult)
            v.tensor_copy(out=pack2[0:1, 0:1], in_=sv["invs1"][0:1, 0:1])
            return v.tensor_copy(out=pack2[0:1, 1:2], in_=sv["as1"][0:1, 0:1])
        ev_sc2 = vemit(_scale)
        pr.wait("tensor", ev_sc2)
        pr.emit("tensor", lambda t: t.drain())
        ev_bc2 = pr.emit("tensor", lambda t: t.matmul(
            ps_b[:, 0:2], ones_f[0:1, :], pack2[0:1, 0:2],
            start=True, stop=True, skip_group_check=True), inc="pe")
        pr.emit("tensor", lambda t: t.drain())
        pr.wait("vector", ev_bc2)

        def _vecs(v):
            v.tensor_copy(out=sv["invs_v"][:, :], in_=ps_b[:, 0:1])
            return v.tensor_copy(out=sv["as_v"][:, :], in_=ps_b[:, 1:2])
        ev_vecs = pr.emit("vector", _vecs, inc="v")

        evC = {}

        def C(i, key):
            return evC.get((i, key))

        for i in range(NT + 1):
            if i < NT:
                xnb = pf[i % 2]
                xmb = pf[2 + i % 2]
                rq = pb[2] if i % 2 == 0 else pb[3]
                tb = ps_s if i % 2 == 0 else ps_b

                pr.wait("sync", C(i - 2, "v_rq"))
                if i <= 1:
                    pr.wait("sync", evB[(NT - 2 + i, "v_y")])
                    pr.wait("sync", evB[(NT - 2 + i, "v_xm")])
                pr.wait("sync", evB[(i, "ds_xn")])
                evC[(i, "dxn")] = pr.emit("sync", lambda s, i=i, xnb=xnb: s.dma_start(
                    out=xnb[:, :], in_=xn_t[i, :, :]), inc="dx", inc_by=16)
                pr.wait("sync", C(i - 2, "v_out"))
                if i <= 1:
                    pr.wait("sync", evB[(NT - 1, "v_xm")])
                    pr.wait("sync", evB[(NT - 1, "v_xn")])
                pr.wait("sync", evB[(i, "ds_xm")])
                evC[(i, "dxm")] = pr.emit("sync", lambda s, i=i, xmb=xmb: s.dma_start(
                    out=xmb[:, :], in_=xm_t[i, :, :]), inc="dx", inc_by=16)

                r1b = pf[6 + i % 2]
                pr.wait("scalar", C(i, "dxn"))
                pr.wait("scalar", ev_vecs)
                pr.wait("scalar", C(i - 2, "s_xqt"))
                evC[(i, "s_zp")] = pr.emit("scalar", lambda s, xnb=xnb, tb=tb: s.activation(
                    out=tb[:, :], in_=xnb[:, :], func=ACT.Identity,
                    scale=sv["invs_v"][:, 0:1], bias=sv["mgk"][:, 0:1]),
                    inc="sc")

                pr.wait("vector", C(i, "s_zp"))
                if i <= 1:
                    pr.wait("vector", evB[(NT - 2 + i, "v_xn")])
                for h in range(2):
                    pr.emit("vector", lambda v, h=h, tb=tb: v.tensor_scalar(
                        out=i32h[:, :], in0=tb[:, h * 512:h * 512 + 512],
                        scalar1=1.0, scalar2=None, op0=ALU.mult))
                    pr.emit("vector", lambda v: v.tensor_copy(
                        out=vflush[:, :], in_=ones128[:, :]))
                    pr.emit("vector", lambda v, h=h, r1b=r1b: v.tensor_copy(
                        out=r1b[:, h * 512:h * 512 + 512], in_=i32h[:, :]))

                pr.emit("vector", lambda v, tb=tb, r1b=r1b, xnb=xnb: v.tensor_tensor(
                    out=xnb[:, :], in0=tb[:, :], in1=r1b[:, :],
                    op=ALU.subtract))
                evC[(i, "v_t1")] = pr.emit(
                    "vector", lambda v, xnb=xnb, r1b=r1b: v.scalar_tensor_tensor(
                        out=xnb[:, :], in0=xnb[:, :], scalar=0.5,
                        in1=r1b[:, :], op0=ALU.is_gt, op1=ALU.add), inc="v")
                pr.wait("vector", C(i - 2, "pe_xqt"))
                if i == 0:
                    pr.wait("vector", evA[(NT - 1, "pe_vt")])
                if i == 1:
                    pr.wait("vector", evB[(NT - 1, "s_msq")])
                evC[(i, "v_rq")] = vemit(lambda v, xnb=xnb, rq=rq: v.tensor_scalar(
                    out=rq[:, :], in0=xnb[:, :], scalar1=256.0, scalar2=None,
                    op0=ALU.subtract))

            for half in range(3):
                if half < 2 and 0 <= i - 1 < NT:
                    fb = ps_a if (i - 1) % 2 == 0 else ps_c
                    xq = pb[0] if (i - 1) % 2 == 0 else pb[1]
                    if half == 0:
                        pr.wait("tensor", C(i - 1, "s_xqt"))
                        pr.wait("tensor", C(i - 3, "v_out"))
                    psl = fb[:, half * 512:half * 512 + 512]
                    ev_f = None
                    for ic in range(NCH):
                        ev_f = pr.emit("tensor", lambda t, psl=psl, xq=xq, ic=ic, half=half: t.matmul(
                            psl, xq[:, ic * P:(ic + 1) * P],
                            sT[:, ic * D + half * 512: ic * D + half * 512 + 512],
                            start=(ic == 0), stop=(ic == NCH - 1),
                            skip_group_check=True),
                            inc="pe" if ic == NCH - 1 else None)
                    evC[(i - 1, f"pe_f{half}")] = ev_f
                if half == 2 and i < NT:
                    pr.wait("tensor", C(i, "v_rq"))
                    pr.wait("tensor", C(i - 2, "s_xqt"))
                    if i == 1:
                        pr.wait("tensor", ev_vecs)
                    ev_tp = None
                    for icb in range(NCH):
                        ev_tp = pr.emit("tensor", lambda t, icb=icb, tb=tb, rq=rq: t.matmul(
                            tb[:, icb * P:(icb + 1) * P],
                            rq[:, icb * P:(icb + 1) * P], idn_bf[:, :],
                            start=True, stop=True, skip_group_check=True),
                            inc="pe")
                    evC[(i, "pe_xqt")] = ev_tp
                    pr.wait("scalar", ev_tp)
                    pr.wait("scalar", C(i - 2, "pe_f1"))
                    xq_i = pb[0] if i % 2 == 0 else pb[1]
                    evC[(i, "s_xqt")] = pr.emit("scalar", lambda s, tb=tb, xq_i=xq_i: s.copy(
                        out=xq_i[:, :], in_=tb[:, :]), inc="sc")

            j = i - 1
            if 0 <= j < NT:
                fb = ps_a if j % 2 == 0 else ps_c
                xmb_j = pf[2 + j % 2]
                ob = pf[4 + j % 2]
                pr.wait("vector", C(j, "pe_f1"))
                pr.wait("vector", C(j, "dxm"))
                pr.wait("vector", C(j - 2, "ds_out"))
                if j <= 1:
                    pr.wait("vector", evB[(NT - 2 + j, "ds_xn")])
                evC[(j, "v_out")] = vemit(
                    lambda v, fb=fb, xmb_j=xmb_j, ob=ob: v.scalar_tensor_tensor(
                        out=ob[:, :], in0=fb[:, :], scalar=sv["as_v"][:, 0:1],
                        in1=xmb_j[:, :], op0=ALU.mult, op1=ALU.add))
                pr.wait("scalar", C(j, "v_out"))
                evC[(j, "ds_out")] = pr.emit("scalar", lambda s, j=j, ob=ob: s.dma_start(
                    out=out_t[j, :, :], in_=ob[:, :]), inc="ds", inc_by=16)

        pr.wait("scalar", ("ds", pr.cnt["ds"]))
        pr.wait("sync", ("dx", pr.cnt["dx"]))
        pr.wait("gpsimd", ("dg", pr.cnt["dg"]))

        @block.sync
        def _(eng):
            pr.run("sync", eng, sems)

        @block.tensor
        def _(eng):
            pr.run("tensor", eng, sems)

        @block.vector
        def _(eng):
            pr.run("vector", eng, sems)

        @block.scalar
        def _(eng):
            pr.run("scalar", eng, sems)

        @block.gpsimd
        def _(eng):
            pr.run("gpsimd", eng, sems)

    return nc


_NC_CACHE = {}


def _get_nc(nt=32):
    if nt not in _NC_CACHE:
        _NC_CACHE[nt] = build_nc(nt)
    return _NC_CACHE[nt]


def kernel(L0_attn_x, L0_attn_Wq, L0_attn_bq, L0_attn_Wk, L0_attn_bk,
           L0_attn_Wv, L0_attn_bv, L0_attn_Wq2, L0_attn_bq2, L0_attn_Wk2,
           L0_attn_bk2, L0_attn_Wv2, L0_attn_bv2, L0_attn_Wo, L0_attn_bo,
           L0_attn_gamma, L0_attn_beta, L0_ffn_weight, L0_ffn_gamma,
           L0_ffn_beta):
    nc = _get_nc(32)
    x = np.ascontiguousarray(np.asarray(L0_attn_x, dtype=np.float32))
    w = {
        "wq": L0_attn_Wq, "wk": L0_attn_Wk, "wv": L0_attn_Wv,
        "wq2": L0_attn_Wq2, "wk2": L0_attn_Wk2, "wv2": L0_attn_Wv2,
        "wo": L0_attn_Wo, "fw": L0_ffn_weight,
        "bq": L0_attn_bq, "bk": L0_attn_bk, "bv": L0_attn_bv,
        "bq2": L0_attn_bq2, "bk2": L0_attn_bk2, "bv2": L0_attn_bv2,
        "bo": L0_attn_bo, "gam": L0_attn_gamma, "bet": L0_attn_beta,
        "gamF": L0_ffn_gamma, "betF": L0_ffn_beta,
    }
    w = {k: np.ascontiguousarray(np.asarray(v, dtype=np.float32))
         for k, v in w.items()}
    in_maps = []
    for c in range(8):
        m = {"x": np.ascontiguousarray(x[c])}
        m.update(w)
        in_maps.append(m)
    res = run_bass_kernel_spmd(nc, in_maps, core_ids=list(range(8)))
    out = np.stack([np.asarray(res.results[c]["out"]) for c in range(8)], axis=0)
    return out.astype(np.float32)
